# version 28
# baseline (speedup 1.0000x reference)
import numpy as np
import ml_dtypes
from contextlib import ExitStack

import concourse.bass as bass
import concourse.mybir as mybir
from concourse.bass_utils import run_bass_kernel_spmd

F32 = mybir.dt.float32
BF16 = mybir.dt.bfloat16
ALU = mybir.AluOpType
ACT = mybir.ActivationFunctionType
AX = mybir.AxisListType.X

D = 2048
T = 4096
NB = 4
HQ = 16
HKV = 4
DH = 128
NKT = 32
NOT_ = 16
OWN = NOT_ * 128
NEG = -1.0e30
MAGIC = 12582912.0
SCALE = 1.0 / float(np.sqrt(DH))
INVD = 1.0 / D
EPS_LN = 1.0e-5


class Ev:
    __slots__ = ("sem", "val")

    def __init__(self, sem, val):
        self.sem = sem
        self.val = val


class Sched:
    def __init__(self):
        self.ops = []
        self.cnt = {}

    def add(self, eng, fn, waits=(), inc=None, n=1):
        ev = None
        if inc is not None:
            self.cnt[inc] = self.cnt.get(inc, 0) + n
            ev = Ev(inc, self.cnt[inc])
        self.ops.append((eng, fn, tuple(w for w in waits if w is not None), inc, n))
        return ev

    def emit(self, eng_name, eng, sems, own_sems):
        water = {}
        for e, fn, waits, inc, n in self.ops:
            if e != eng_name:
                continue
            for w in waits:
                if w.sem in own_sems:
                    continue
                if water.get(w.sem, -1) >= w.val:
                    continue
                eng.wait_ge(sems[w.sem], w.val)
                water[w.sem] = w.val
            ins = fn(eng)
            if inc is not None:
                ins.then_inc(sems[inc], n)


def build_program(debug=False):
    nc = bass.Bass("TRN2", target_bir_lowering=False, debug=False)

    xk = nc.dram_tensor("xk", [T, D], F32, kind="ExternalInput").ap()
    xo = nc.dram_tensor("xo", [OWN, D], F32, kind="ExternalInput").ap()
    wq_t = nc.dram_tensor("wq_t", [D, D], F32, kind="ExternalInput").ap()
    wkv_t = nc.dram_tensor("wkv_t", [128, 16 * 1024], F32, kind="ExternalInput").ap()
    wp_t = nc.dram_tensor("wp_t", [D, D], F32, kind="ExternalInput").ap()
    wf_t = nc.dram_tensor("wf_t", [D, D], F32, kind="ExternalInput").ap()
    ropeq_cc = nc.dram_tensor("ropeq_cc", [128, OWN], F32, kind="ExternalInput").ap()
    ropeq_ss = nc.dram_tensor("ropeq_ss", [128, OWN], F32, kind="ExternalInput").ap()
    ropek = nc.dram_tensor("ropek", [128, NKT * 192], F32, kind="ExternalInput").ap()
    dmaskd = nc.dram_tensor("dmaskd", [128, NOT_ * 256], F32, kind="ExternalInput").ap()
    onescol = nc.dram_tensor("onescol", [128, 1], F32, kind="ExternalInput").ap()
    onesrow = nc.dram_tensor("onesrow", [1, 128], F32, kind="ExternalInput").ap()
    identin = nc.dram_tensor("identin", [128, 128], F32, kind="ExternalInput").ap()
    y = nc.dram_tensor("y", [OWN, D], F32, kind="ExternalOutput").ap()
    if debug:
        d_kf = nc.dram_tensor("d_kf", [128, HKV * T], F32, kind="ExternalOutput").ap()
        d_vt = nc.dram_tensor("d_vt", [128, NKT * 512], F32, kind="ExternalOutput").ap()
        d_q = nc.dram_tensor("d_q", [128, HQ * OWN], F32, kind="ExternalOutput").ap()
        d_negb = nc.dram_tensor("d_negb", [128, 4 * HQ], F32, kind="ExternalOutput").ap()
        d_attn = nc.dram_tensor("d_attn", [128, HQ * OWN], F32, kind="ExternalOutput").ap()
        d_x2a = nc.dram_tensor("d_x2a", [128, 16384], F32, kind="ExternalOutput").ap()
        d_al = nc.dram_tensor("d_al", [128, 1], F32, kind="ExternalOutput").ap()
        d_alr = nc.dram_tensor("d_alr", [128, 1], F32, kind="ExternalOutput").ap()
        d_W = nc.dram_tensor("d_W", [128, 24], F32, kind="ExternalOutput").ap()
        d_x1 = nc.dram_tensor("d_x1", [128, 2048], F32, kind="ExternalOutput").ap()
        d_asum = nc.dram_tensor("d_asum", [128, 1], F32, kind="ExternalOutput").ap()
        d_xb = nc.dram_tensor("d_xb", [128, 2048], F32, kind="ExternalOutput").ap()
        d_st = nc.dram_tensor("d_st", [128, 8], F32, kind="ExternalOutput").ap()
        d_xq = nc.dram_tensor("d_xq", [128, 16384], F32, kind="ExternalOutput").ap()
        d_beta = nc.dram_tensor("d_beta", [128, 8], F32, kind="ExternalOutput").ap()
        d_wq = nc.dram_tensor("d_wq", [128, 2048], F32, kind="ExternalOutput").ap()
        d_ev = nc.dram_tensor("d_ev", [128, 1024], F32, kind="ExternalOutput").ap()

    sch = Sched()
    ctx = ExitStack()
    E = ctx.enter_context

    qa = E(nc.sbuf_tensor("qa", [128, HQ * OWN], BF16))
    kf_h = nc.alloc_sbuf_tensor("kf", [128, HKV * T], BF16)
    vt_h = nc.alloc_sbuf_tensor("vt", [128, NKT * 512], BF16)
    kf = kf_h.ap()
    vt = vt_h.ap()
    kf_addr = nc.lookup_mloc(kf_h).addr
    x2f = nc.alloc_sbuf_tensor_at("x2f", [128, 16384], F32, offset=kf_addr).ap()
    a3 = E(nc.sbuf_tensor("a3", [128, 16384], BF16))
    xbuf_h = nc.alloc_sbuf_tensor("xbuf", [128, 2 * D], F32)
    xbuf = xbuf_h.ap()
    xbufb = nc.alloc_sbuf_tensor_at(
        "xbufb", [128, 4 * D], BF16,
        offset=nc.lookup_mloc(xbuf_h).addr).ap()
    wqs = [E(nc.sbuf_tensor("wqs0", [128, 2048], BF16)),
           E(nc.sbuf_tensor("wqs1", [128, 2048], BF16))]
    tab8 = E(nc.sbuf_tensor("tab8", [128, 4096], BF16))
    evsb = E(nc.sbuf_tensor("evsb", [128, 2048], F32))
    rsig = evsb[:, 0:1024]
    idb = E(nc.sbuf_tensor("idb", [128, 128], BF16))
    idf = E(nc.sbuf_tensor("idf", [128, 128], F32))
    onec = E(nc.sbuf_tensor("onec", [128, 1], BF16))
    onec32 = E(nc.sbuf_tensor("onec32", [128, 1], F32))
    oner = E(nc.sbuf_tensor("oner", [1, 128], BF16))
    oner32 = E(nc.sbuf_tensor("oner32", [1, 128], F32))
    epsv = E(nc.sbuf_tensor("epsv", [128, 1], F32))
    st_ln = [E(nc.sbuf_tensor(f"st_ln{i}", [128, 1], F32)) for i in range(2)]
    st_rs = [E(nc.sbuf_tensor(f"st_rs{i}", [128, 1], F32)) for i in range(2)]
    bnst = [E(nc.sbuf_tensor(f"bnst{i}", [128, 24], F32)) for i in range(2)]
    bnagg = [E(nc.sbuf_tensor(f"bnagg{i}", [128, 2], F32)) for i in range(2)]
    negb = E(nc.sbuf_tensor("negb", [128, 4 * HQ], F32))
    qmax64 = E(nc.sbuf_tensor("qmax64", [128, 4 * HQ], F32))
    kmax64 = E(nc.sbuf_tensor("kmax64", [128, 4 * HQ], F32))
    bln64 = E(nc.sbuf_tensor("bln64", [128, 4 * HQ], F32))

    asum = E(nc.sbuf_tensor("asum", [128, 1], F32))
    asum_t = E(nc.sbuf_tensor("asum_t", [128, 1], F32))
    atot = E(nc.sbuf_tensor("atot1", [1, 1], F32))
    al_a = E(nc.sbuf_tensor("al_a", [128, 1], F32))
    al_r = E(nc.sbuf_tensor("al_r", [128, 1], F32))
    bmax2 = [E(nc.sbuf_tensor(f"bmax{i}", [128, 1], F32)) for i in range(2)]
    beta8 = E(nc.sbuf_tensor("beta8", [128, 8], F32))
    dstats = E(nc.sbuf_tensor("dstats", [128, 8], F32))
    gapt = E(nc.sbuf_tensor("gapt", [128, 1], F32))
    gapt_s = E(nc.sbuf_tensor("gapt_s", [128, 1], F32))
    gsrc_s = E(nc.sbuf_tensor("gsrc_s", [128, 1], F32))
    gsrc_v = E(nc.sbuf_tensor("gsrc_v", [128, 1], F32))
    accA = E(nc.sbuf_tensor("accA", [128, 1], F32))
    accB = E(nc.sbuf_tensor("accB", [128, 1], F32))
    betr8 = E(nc.sbuf_tensor("betr8", [128, 8], F32))

    PA = E(nc.psum_tensor("PA", [128, 1024], F32))
    PB = E(nc.psum_tensor("PB", [128, 1024], F32))
    PC = E(nc.psum_tensor("PC", [128, 1024], F32))
    PTB = E(nc.psum_tensor("PTB", [128, 1024], BF16))
    PTB2 = E(nc.psum_tensor("PTB2", [128, 1024], BF16))

    sems = {}
    for s in ["te", "ve", "se", "sts", "sts2", "ldc", "lds", "ldq", "lw0", "lw1",
              "lg0", "lg1", "lx0", "lx1", "lx2", "lx3", "lxp", "lp0", "lp1",
              "dbg"]:
        sems[s] = E(nc.semaphore(s))

    def dg(sem, out, in_, waits=()):
        return sch.add("gpsimd", lambda e, o=out, i=in_: e.dma_start(out=o, in_=i),
                       waits, inc=sem, n=16)

    def ds(sem, out, in_, waits=()):
        return sch.add("sync", lambda e, o=out, i=in_: e.dma_start(out=o, in_=i),
                       waits, inc=sem, n=16)

    def dst(out, in_, waits=()):
        return sch.add("sync", lambda e, o=out, i=in_: e.dma_start(out=o, in_=i),
                       waits, inc="sts", n=16)

    def dsc(sem, out, in_, waits=()):
        return sch.add("scalar", lambda e, o=out, i=in_: e.dma_start(out=o, in_=i),
                       waits, inc=sem, n=16)

    def vgap():
        ve(lambda e: e.tensor_copy(gapt[:], onec32[:]), [ev_onec32])

    def sgap():
        se(lambda e: e.copy(gapt_s[:], onec32[:]), [ev_onec32])

    def te(fn, waits=()):
        return sch.add("tensor", fn, waits, inc="te")

    def ve(fn, waits=()):
        return sch.add("vector", fn, waits, inc="ve")

    def se(fn, waits=()):
        return sch.add("scalar", fn, waits, inc="se")

    ev_idb = dg("ldc", idb[:], identin)
    ev_idf = ds("lds", idf[:], identin)
    ev_onec = dg("ldc", onec[:], onescol)
    ev_onec32 = ds("lds", onec32[:], onescol)
    ev_oner = dg("ldc", oner[:], onesrow)
    ev_oner32 = ds("lds", oner32[:], onesrow)
    ev_wkv = dg("ldc", qa[:, 0:16 * 1024], wkv_t)
    ev_ropek = dg("ldc", a3[:, 8192:8192 + NKT * 192], ropek)
    ev_ldc_all = Ev("ldc", sch.cnt["ldc"])
    ev_lds_all = Ev("lds", sch.cnt["lds"])
    ev_idb = ev_onec = ev_oner = ev_wkv = ev_ropek = ev_ldc_all
    ev_idf = ev_onec32 = ev_oner32 = ev_lds_all
    ev_eps = ve(lambda e: e.memset(epsv[:], EPS_LN))

    from concourse.hw_specs import get_activation_tables

    def _preload(e):
        for _i, (_n, _f) in enumerate(get_activation_tables(nc.m.arch).items()):
            if _n == "natural_log_exp_and_others":
                return e.add_instruction(mybir.InstLoadActFuncSet(
                    name=nc.make_name("acttab", add_next_id=True),
                    act_func_set_id=_i, ins=[], outs=[]))
        raise RuntimeError("no natural_log_exp_and_others table set")
    se(_preload)

    def emit_ln(buf, src_ap, ev_src, out_ap, out_waits=()):
        w = [ev_src, ev_eps]
        e1 = None
        for c in range(4):
            e1 = ve(lambda e, c=c: e.bn_stats(
                bnst[buf][:, 6 * c:6 * (c + 1)],
                src_ap[:, 512 * c:512 * (c + 1)]), w if c == 0 else [])
        vgap()
        e5 = ve(lambda e: e.bn_aggr(bnagg[buf][:], bnst[buf][:]), [e1])
        e6 = se(lambda e: e.activation(st_ln[buf][:], bnagg[buf][:, 1:2], ACT.Ln,
                                       bias=epsv[:]), [e5])
        sgap()
        e7 = se(lambda e: e.activation(st_rs[buf][:], st_ln[buf][:], ACT.Exp,
                                       scale=-0.5), [e6])
        vgap()
        e8 = ve(lambda e: e.tensor_scalar(out_ap, src_ap, bnagg[buf][:, 0:1],
                                          st_rs[buf][:], ALU.subtract, ALU.mult),
                [e7] + list(out_waits))
        return e8

    SQ_A3 = a3[:, 14336:16384]

    pkv = [PA, PB]
    ev_lnrd = [None, None]
    ev_kvmm = [None, None]
    ev_kv_free = [None, None]
    ev_x1tm_free = [None, None]
    ev_x1fm_free = [None, None]
    ptbs = [PTB, PTB2]
    ev_ptbf = [None, None]
    pidx = [0]

    def ptb_next():
        b = pidx[0] % 2
        pidx[0] += 1
        return b
    ev_vt_last = None
    ev_kf_last = None
    ev_p1_te_last = None

    ev_lnx = [None] * 4
    ev_ln_t = {}
    ev_trd = {}
    X1TM_BASE = [0, 2048, 14336]
    ev_x1tm_free3 = [None, None, None]

    def p1_ln(kt):
        slot3 = kt % 3
        s3 = kt % 4
        ev_x = dg(f"lx{s3}", xbufb[:, s3 * D:(s3 + 1) * D],
                  xk[kt * 128:(kt + 1) * 128, :], [ev_lnx[s3]])
        ev_ln = emit_ln(kt % 2, xbufb[:, s3 * D:(s3 + 1) * D], ev_x,
                        a3[:, X1TM_BASE[slot3]:X1TM_BASE[slot3] + 2048],
                        [ev_x1tm_free3[slot3]])
        ev_lnx[s3] = ev_ln
        ev_lnrd[kt % 2] = ev_ln
        ev_ln_t[kt] = ev_ln

    def p1_tr(kt):
        buf = kt % 2
        base = X1TM_BASE[kt % 3]
        for rnd in range(2):
            pb = ptb_next()
            last_t = None
            for i in range(8):
                w = [ev_ln_t[kt], ev_ptbf[pb]] if i == 0 else []
                last_t = te(lambda e, i=i, rnd=rnd, b=base, pb=pb: e.transpose(
                    ptbs[pb][:, i * 128:(i + 1) * 128],
                    a3[:, b + (rnd * 8 + i) * 128:b + (rnd * 8 + i + 1) * 128],
                    idb[:]), w + ([ev_idb] if kt == 0 and rnd == 0 else []))
            ev_ptbf[pb] = se(lambda e, rnd=rnd, buf=buf, pb=pb: e.activation(
                a3[:, 4096 + buf * 2048 + rnd * 1024:
                   4096 + buf * 2048 + (rnd + 1) * 1024], ptbs[pb][:], ACT.Copy),
                [last_t, ev_x1fm_free[buf] if rnd == 0 else None])
        ev_trd[kt] = ev_ptbf[(pidx[0] - 1) % 2]

    ev_m3_t = {}

    def k_tr(kt):
        nonlocal ev_p1_te_last, ev_kf_last
        base = X1TM_BASE[kt % 3]
        kh = a3[:, base + 1024:base + 1536]
        pb = ptb_next()
        last_t = None
        for i in range(4):
            w = [ev_m3_t[kt], ev_ptbf[pb]] if i == 0 else []
            last_t = te(lambda e, i=i, kh=kh, pb=pb: e.transpose(
                ptbs[pb][:, i * 128:(i + 1) * 128], kh[:, i * 128:(i + 1) * 128],
                idb[:]), w)
        ev_x1tm_free3[kt % 3] = last_t
        ev_p1_te_last = last_t
        kfdst = kf.rearrange("p (hk r) -> p hk r", hk=HKV)[:, :, kt * 128:(kt + 1) * 128]
        ev_ptbf[pb] = se(lambda e, kfdst=kfdst, pb=pb: e.activation(
            kfdst, ptbs[pb][:, 0:512].rearrange("p (hk r) -> p hk r", hk=4),
            ACT.Copy), [last_t])
        ev_kf_last = ev_ptbf[pb]

    p1_ln(0)
    p1_ln(1)
    p1_tr(0)
    for kt in range(NKT):
        if kt + 1 < NKT:
            p1_tr(kt + 1)
        if kt >= 1:
            k_tr(kt - 1)
        if kt + 2 < NKT:
            p1_ln(kt + 2)
        buf = kt % 2
        base = X1TM_BASE[kt % 3]
        ev_x1fm = ev_trd[kt]
        for d in range(16):
            x1fm_d = a3[:, 4096 + buf * 2048 + d * 128:4096 + buf * 2048 + (d + 1) * 128]
            w = [ev_x1fm, ev_wkv, ev_kv_free[buf]] if d == 0 else []
            te(lambda e, d=d, x1=x1fm_d, P=pkv[buf]: e.matmul(
                P[:, 0:512], x1, qa[:, d * 1024:d * 1024 + 512],
                start=(d == 0), stop=(d == 15), skip_group_check=True), w)
            ev_kvmm[buf] = te(lambda e, d=d, x1=x1fm_d, P=pkv[buf]: e.matmul(
                P[:, 512:1024], x1, qa[:, d * 1024 + 512:(d + 1) * 1024],
                start=(d == 0), stop=(d == 15), skip_group_check=True), [])
        ev_vt_last = se(lambda e, kt=kt, P=pkv[buf]: e.activation(
            vt[:, kt * 512:(kt + 1) * 512], P[:, 512:1024], ACT.Copy),
            [ev_kvmm[buf]])
        ka = a3[:, base + 0:base + 512]
        kh = a3[:, base + 1024:base + 1536]
        ksw = a3[:, base + 1536:base + 2048]
        kev = pkv[buf][:, 0:512]
        r4 = lambda ap: ap.rearrange("p (h two f) -> p h two f", h=4, two=2)
        sw1 = ve(lambda e, ksw=ksw, kev=kev: e.tensor_copy(
            r4(ksw)[:, :, 0, :], r4(kev)[:, :, 1, :]), [ev_kvmm[buf]])
        ve(lambda e, ksw=ksw, kev=kev: e.tensor_copy(
            r4(ksw)[:, :, 1, :], r4(kev)[:, :, 0, :]), [])
        kb_t = kt * 192 + 8192
        tcc = a3[:, kb_t:kb_t + 64].rearrange("p (a b f) -> p a b f", a=1, b=1) \
            .broadcast_to([128, 4, 2, 64])
        tss = a3[:, kb_t + 64:kb_t + 192].rearrange("p (a two f) -> p a two f",
                                                    a=1, two=2) \
            .broadcast_to([128, 4, 2, 64])
        ev_m1 = ve(lambda e, ka=ka, kev=kev, tcc=tcc: e.tensor_tensor(
            r4(ka), r4(kev), tcc, ALU.mult), [ev_ropek] if kt == 0 else [])
        ev_kv_free[buf] = ev_m1
        ev_m2 = ve(lambda e, ksw=ksw, tss=tss: e.tensor_tensor(
            r4(ksw), r4(ksw), tss, ALU.mult), [])
        vgap()
        ev_m3_t[kt] = ve(lambda e, kh=kh, ka=ka, ksw=ksw: e.tensor_tensor(
            kh[:], ka[:], ksw[:], ALU.add), [ev_m2])
        ev_x1fm_free[buf] = ev_kvmm[buf]
    k_tr(NKT - 1)
    ev_x1tm_free = [ev_x1tm_free3[0], ev_x1tm_free3[1]]

    ev_wkv_rd = ev_kvmm[(NKT - 1) % 2]
    ev_dbg_kv = None
    if debug:
        e1 = dg("dbg", d_kf, kf, [ev_kf_last, ev_vt_last])
        ev_dbg_kv = dg("dbg", d_vt, vt, [e1])

    ev_dbg_x1 = None
    if debug:
        ev_dbg_x1 = dg("dbg", d_x1, a3[:, 2048:4096], [ev_lnrd[1]])

    krow = E(nc.sbuf_tensor("krow", [128, 32], F32))
    lnrow = [E(nc.sbuf_tensor(f"lnrow{i}", [1, 512], F32)) for i in range(2)]
    kmax128 = [E(nc.sbuf_tensor(f"kmax128_{i}", [128, 1], F32)) for i in range(HKV)]
    dbgW = E(nc.sbuf_tensor("dbgW", [128, 24], F32))
    nsq_ring = [a3[:, 14336 + i * 512:14336 + (i + 1) * 512] for i in range(4)]
    ev_ring_mm = [None] * 4
    ev_row_ps = None
    ev_bank1_free = None
    ev_sq_u = {}
    nunits = []
    for g_ in range(16):
        nunits.append(('q', g_, 0))
        nunits.append(('k', 2 * g_ // 8, (2 * g_) % 8))
        nunits.append(('k', (2 * g_ + 1) // 8, (2 * g_ + 1) % 8))
    for qq_ in range(1, 4):
        for h_ in range(16):
            nunits.append(('q', h_, qq_))
    n_sq = [0]
    n_ch = [0]

    def emit_sq(extra=(), src=None, ascale=1.0):
        u = n_sq[0]
        slot = u % 4
        kind, a, b = nunits[u]
        if src is None:
            src = kf[:, a * T + b * 512:(a * T + (b + 1) * 512)] if kind == 'k' \
                else qa[:, a * OWN + b * 512:a * OWN + (b + 1) * 512]
        w = [ev_ring_mm[slot]] + list(extra)
        if kind == 'k':
            w.append(ev_kf_last)
        if u < 4:
            w.append(ev_x1tm_free3[2])
        ev_sq_u[u] = se(lambda e, slot=slot, src=src, ascale=ascale: e.activation(
            nsq_ring[slot], src, ACT.Square, scale=ascale), w)
        n_sq[0] = u + 1
        return ev_sq_u[u]

    def emit_chain():
        nonlocal ev_row_ps, ev_bank1_free
        u = n_ch[0]
        slot = u % 4
        ev_mm = te(lambda e, slot=slot: e.matmul(
            PC[0:1, 0:512], onec[:], nsq_ring[slot],
            start=True, stop=True), [ev_sq_u.pop(u), ev_onec, ev_row_ps])
        ev_cp = ve(lambda e, slot=slot: e.tensor_copy(
            nsq_ring[slot][0:1, :], PC[0:1, 0:512]), [ev_mm])
        ev_row_ps = ev_cp
        ev_bc = te(lambda e, slot=slot: e.matmul(
            PC[:, 512:1024], oner[:], nsq_ring[slot][0:1, :],
            start=True, stop=True), [ev_cp, ev_oner, ev_bank1_free])
        ev_ring_mm[slot] = ev_bc
        kind, a, b = nunits[u]
        dst = krow[:, 8 * a + b:8 * a + b + 1] if kind == 'k' \
            else qmax64[:, 4 * a + b:4 * a + b + 1]
        ev_bank1_free = ve(lambda e, dst=dst: e.reduce_max(
            dst, PC[:, 512:1024], AX), [ev_bc])
        n_ch[0] = u + 1

    def catch_chains(target):
        while n_ch[0] < min(target, n_sq[0]):
            emit_chain()

    ev_qslab = [None, None]
    ev_qslab_rd = [None, None]
    ev_qpsum_free = [None] * 4
    ev_qtmp_free = [None, None]
    ev_rqcc = dg("ldq", tab8[:, 0:2048], ropeq_cc, [])
    ev_rqss = dg("ldq", tab8[:, 2048:4096], ropeq_ss, [])
    ev_qa_w = {}
    ev_x1qfm_free = None
    ev_last_qmm = None
    qslot = [(PA, 0), (PA, 512), (PB, 0), (PB, 512)]

    ev_p2ln = {}
    ev_x1tm_rd = [None, None]

    def p2_ln(tile, extra=()):
        buf = tile % 2
        s3 = tile % 4
        ev_x = dg(f"lx{s3}", xbufb[:, s3 * D:(s3 + 1) * D],
                  xo[tile * 128:(tile + 1) * 128, :], [ev_lnx[s3]])
        ev_ln = emit_ln(buf, xbufb[:, s3 * D:(s3 + 1) * D], ev_x,
                        a3[:, buf * 2048:buf * 2048 + 2048],
                        [ev_x1tm_rd[buf]] + list(extra))
        ev_lnx[s3] = ev_ln
        ev_lnrd[buf] = ev_ln
        ev_p2ln[tile] = ev_ln

    for qq in range(4):
        ev_tile_fm = None
        for tq in range(4):
            tile = qq * 4 + tq
            buf = tile % 2
            x1tm_base = buf * 2048
            if tile not in ev_p2ln:
                p2_ln(tile, [ev_x1tm_free[buf], ev_dbg_x1]
                      if qq == 0 and tq < 2 else [])
            ev_ln = ev_p2ln[tile]
            last_tr = None
            for rnd in range(2):
                pb = ptb_next()
                last_t = None
                for i in range(8):
                    w = [ev_ln, ev_ptbf[pb]] if i == 0 else []
                    if qq == 0 and tq == 0 and rnd == 0 and i == 0:
                        w.append(ev_p1_te_last)
                    last_t = te(lambda e, i=i, rnd=rnd, b=x1tm_base, pb=pb:
                                e.transpose(
                        ptbs[pb][:, i * 128:(i + 1) * 128],
                        a3[:, b + (rnd * 8 + i) * 128:b + (rnd * 8 + i + 1) * 128],
                        idb[:]), w)
                last_tr = last_t
                dst_ap = a3[:, 4096 + rnd * 4096:4096 + (rnd + 1) * 4096] \
                    .rearrange("p (i c) -> p i c", c=512)[:, :, tq * 128:(tq + 1) * 128]
                wv = [last_t]
                if qq > 0 and tq == 0 and rnd == 0:
                    wv.append(ev_x1qfm_free)
                ev_tile_fm = se(lambda e, dst=dst_ap, pb=pb: e.activation(
                    dst, ptbs[pb][:].rearrange("p (i c) -> p i c", c=128),
                    ACT.Copy), wv)
                ev_ptbf[pb] = ev_tile_fm
            ev_x1tm_rd[buf] = last_tr
        for h in range(HQ):
            sb = h % 2
            ev_qslab[sb] = dg(
                f"lw{sb}", wqs[sb][:], wq_t[h * 128:(h + 1) * 128, :],
                [ev_qslab_rd[sb]])
            P, off = qslot[h % 4]
            last = None
            for d in range(16):
                w = [ev_qslab[sb], ev_tile_fm, ev_qpsum_free[h % 4]] if d == 0 else []
                last = te(lambda e, d=d, P=P, off=off, sb=sb: e.matmul(
                    P[:, off:off + 512], wqs[sb][:, d * 128:(d + 1) * 128],
                    a3[:, 4096 + d * 512:4096 + (d + 1) * 512],
                    start=(d == 0), stop=(d == 15), skip_group_check=True), w)
            ev_qslab_rd[sb] = last
            ev_last_qmm = last
            qb = h % 2
            qt_base = 12288 + qb * 1024
            qtmp = a3[:, qt_base:qt_base + 512]
            qsw = a3[:, qt_base + 512:qt_base + 1024]
            ev_ev = se(lambda e, qtmp=qtmp, P=P, off=off: e.activation(
                qtmp, P[:, off:off + 512], ACT.Copy), [last, ev_qtmp_free[qb]])
            ev_qpsum_free[h % 4] = emit_sq([last], src=P[:, off:off + 512],
                                           ascale=SCALE)
            if qq == 0:
                emit_sq()
                emit_sq()
            catch_chains(n_sq[0] - 3)
            if h >= 14 and qq < 3:
                p2_ln(4 * (qq + 1) + (h - 14))
            s1 = ve(lambda e, qsw=qsw, qtmp=qtmp: e.tensor_copy(
                qsw[0:64, :], qtmp[64:128, :]), [ev_ev, ev_rqcc, ev_rqss])
            ve(lambda e, qsw=qsw, qtmp=qtmp: e.tensor_copy(
                qsw[64:128, :], qtmp[0:64, :]), [])
            ve(lambda e, qtmp=qtmp, qq=qq: e.tensor_tensor(
                qtmp[:], qtmp[:], tab8[:, qq * 512:(qq + 1) * 512], ALU.mult), [])
            ve(lambda e, qsw=qsw, qq=qq: e.tensor_tensor(
                qsw[:], qsw[:], tab8[:, 2048 + qq * 512:2048 + (qq + 1) * 512],
                ALU.mult), [])
            vgap()
            ev_rope = ve(lambda e, h=h, qq=qq, qtmp=qtmp, qsw=qsw: e.tensor_tensor(
                qa[:, h * OWN + qq * 512:h * OWN + (qq + 1) * 512],
                qtmp[:], qsw[:], ALU.add), [ev_wkv_rd])
            ev_qtmp_free[qb] = ev_rope
            ev_qa_w[(h, qq)] = ev_rope
        ev_x1qfm_free = ev_last_qmm

    catch_chains(96)
    ev_km = []
    for hk in range(HKV):
        ev_km.append(ve(lambda e, hk=hk: e.reduce_max(
            kmax128[hk][:], krow[:, 8 * hk:8 * (hk + 1)], AX), []))
    vgap()
    ev_kx = []
    for hk in range(HKV):
        ev_kx.append(ve(lambda e, hk=hk: e.tensor_copy(
            kmax64[:, 16 * hk:16 * (hk + 1)],
            kmax128[hk][:].broadcast_to([128, 16])), [ev_km[hk]]))
        ve(lambda e, hk=hk: e.tensor_copy(
            dbgW[:, 16 + hk:17 + hk], kmax128[hk][:]), [])
    vgap()
    eb1 = ve(lambda e: e.tensor_tensor(bln64[:], qmax64[:], kmax64[:], ALU.mult),
             ev_kx)
    eb2 = se(lambda e: e.activation(bln64[:], bln64[:], ACT.Ln), [eb1])
    sgap()
    eb3 = se(lambda e: e.activation(negb[:], bln64[:], ACT.Exp, scale=0.5), [eb2])
    ev_negb = ve(lambda e: e.tensor_scalar(negb[:], negb[:], -1.0, 38.0,
                                           ALU.mult, ALU.add), [eb3])

    ev_dbg_q = None
    if debug:
        e1 = dg("dbg", d_q, qa[:, :], [ev_qa_w[(HQ - 1, 3)]])
        e1 = dg("dbg", d_W, dbgW[:], [ev_negb, e1])
        ev_dbg_q = dg("dbg", d_negb, negb[:], [e1])

    ev_albuf_rd = [None, None]
    ev_acc = None
    ev_alw = [None] * 16
    n_aliss = [0]
    n_alred = [0]

    def alpha_issue():
        sl = n_aliss[0]
        n_aliss[0] = sl + 1
        buf = sl % 2
        ev_alw[sl] = dg(f"lg{buf}", xbuf[:, buf * D:(buf + 1) * D],
                        wf_t[sl * 128:(sl + 1) * 128, :],
                        [ev_albuf_rd[buf]] if sl >= 2 else list(ev_lnx))

    def alpha_reduce():
        sl = n_alred[0]
        n_alred[0] = sl + 1
        nonlocal ev_acc
        buf = sl % 2
        ev_r = ve(lambda e, buf=buf: e.reduce_sum(
            asum_t[:], xbuf[:, buf * D:(buf + 1) * D], AX,
            apply_absolute_value=True), [ev_alw[sl]])
        ev_albuf_rd[buf] = ev_r
        vgap()
        if sl == 0:
            ev_acc = ve(lambda e: e.tensor_copy(accA[:], asum_t[:]), [ev_r])
        elif sl % 2 == 1:
            dst_ = asum[:] if sl == 15 else accB[:]
            ev_acc = ve(lambda e, dst_=dst_: e.tensor_tensor(
                dst_, accA[:], asum_t[:], ALU.add), [ev_r, ev_acc])
        else:
            ev_acc = ve(lambda e: e.tensor_tensor(accA[:], accB[:], asum_t[:],
                                                  ALU.add), [ev_r, ev_acc])

    ev_dm = dg("ldq", tab8[:, 0:4096], dmaskd,
               [ev_qa_w[(HQ - 1, 3)], ev_rqcc, ev_rqss])
    ev_entry = [ev_kf_last, ev_negb, ev_dm, ev_qpsum_free[0], ev_qpsum_free[1],
                ev_qpsum_free[2], ev_qpsum_free[3], ev_row_ps, ev_bank1_free,
                ev_onec, ev_oner]

    def stream_gen(t):
        pv = PA[:, t * 512:(t + 1) * 512]
        stp = PB[:, t * 512:(t + 1) * 512]
        sg = PC[0:1, t * 512:(t + 1) * 512]
        sgrow = a3[0:1, 3072 + t * 512:3072 + (t + 1) * 512]
        tln = evsb[:, 1024 + t * 512:1024 + (t + 1) * 512]
        rsg = evsb[:, t * 512:(t + 1) * 512]
        ev_st_free = None
        ev_pv_free = None
        ev_sg_free = None
        ev_ptfree = [None, None]
        first = True
        for hh in range(8):
            for s4 in range(4):
                h = 2 * hh + t
                hk = h // 4
                ktmax = 8 * (s4 + 1)
                ev_exp = None
                ev_pend = None
                ev_pv_last = None
                ev_sg_last = None
                for kt in range(ktmax):
                    jm = kt // 2
                    c0 = max(jm * 128 - s4 * 512, 0)
                    W = 512 - c0
                    w = [ev_exp if kt > 0 else ev_st_free]
                    if first and kt == 0:
                        w += ev_entry
                    ev_st = te(lambda e, h=h, hk=hk, kt=kt, s4=s4, c0=c0, stp=stp:
                               e.matmul(
                        stp[:, c0:512],
                        kf[:, hk * T + kt * 128:hk * T + (kt + 1) * 128],
                        qa[:, h * OWN + s4 * 512 + c0:h * OWN + (s4 + 1) * 512],
                        start=True, stop=True, skip_group_check=True), w)
                    ev_msk = None
                    if kt >= 8 * s4:
                        mc = jm * 128 - s4 * 512
                        dmc = jm * 256 + (kt - 2 * jm) * 128
                        ev_msk = ve(lambda e, mc=mc, dmc=dmc, stp=stp:
                                    e.tensor_tensor(
                            stp[:, mc:mc + 128], stp[:, mc:mc + 128],
                            tab8[:, dmc:dmc + 128], ALU.add), [ev_st])
                    ps = kt % 2
                    ptap = a3[:, t * 1024 + ps * 512:t * 1024 + ps * 512 + W]
                    ev_exp = se(lambda e, ptap=ptap, c0=c0, h=h, s4=s4, stp=stp:
                                e.activation(
                        ptap, stp[:, c0:512], ACT.Exp,
                        bias=negb[:, 4 * h + s4:4 * h + s4 + 1]),
                        [ev_msk if ev_msk is not None else ev_st, ev_ptfree[ps]])
                    if ev_pend is not None:
                        emit_pvsig(*ev_pend)
                    ev_pend = (t, h, hk, kt, c0, W, ps, ev_exp,
                               kt == 0, kt == ktmax - 1)

                    def emit_pvsig(t_, h_, hk_, kt_, c0_, W_, ps_, ev_e, st0, stN):
                        nonlocal ev_pv_last, ev_sg_last
                        w1 = [ev_e]
                        if st0:
                            w1 += [ev_pv_free]
                        ev_pv_last = te(lambda e: e.matmul(
                            PA[:, t_ * 512 + c0_:(t_ + 1) * 512],
                            vt[:, kt_ * 512 + hk_ * 128:kt_ * 512 + (hk_ + 1) * 128],
                            a3[:, t_ * 1024 + ps_ * 512:t_ * 1024 + ps_ * 512 + W_],
                            start=st0, stop=stN, skip_group_check=True), w1)
                        w2 = [ev_sg_free] if st0 else []
                        ev_sg_last = te(lambda e: e.matmul(
                            PC[0:1, t_ * 512 + c0_:(t_ + 1) * 512], onec[:],
                            a3[:, t_ * 1024 + ps_ * 512:t_ * 1024 + ps_ * 512 + W_],
                            start=st0, stop=stN, skip_group_check=True), w2)
                        ev_ptfree[ps_] = ev_sg_last
                    yield
                emit_pvsig(*ev_pend)
                ev_pend = None
                first = False
                n0 = ve(lambda e, tln=tln, sg=sg: e.tensor_scalar(
                    tln[0:1, :], sg, 1.0, 1.0e-36, ALU.mult, ALU.add),
                        [ev_sg_last])
                n1a = se(lambda e, tln=tln, t=t: e.activation(
                    lnrow[t][:], tln[0:1, :], ACT.Ln), [n0])
                sgap()
                n1 = se(lambda e, rsg=rsg, t=t: e.activation(
                    rsg[0:1, :], lnrow[t][:], ACT.Exp, scale=-1.0), [n1a])
                n2 = te(lambda e, stp=stp, rsg=rsg: e.matmul(
                    stp[:], oner32[:], rsg[0:1, :], start=True, stop=True),
                        [n1, ev_exp, ev_oner32])
                n3 = ve(lambda e, rsg=rsg, stp=stp: e.tensor_copy(rsg, stp[:]),
                        [n2])
                vgap()
                n5 = ve(lambda e, h=h, s4=s4, pv=pv, rsg=rsg: e.tensor_tensor(
                    qa[:, h * OWN + s4 * 512:h * OWN + (s4 + 1) * 512],
                    pv, rsg, ALU.mult), [n3, ev_pv_last])
                ev_attn_w[(h, s4)] = n5
                ev_st_free = n3
                ev_pv_free = n5
                ev_sg_free = n0
                yield

    ev_attn_w = {}
    gens = [stream_gen(0), stream_gen(1)]
    done = [False, False]
    for _ in range(5):
        next(gens[0])
    it = 0
    while not all(done):
        for t in range(2):
            if not done[t]:
                try:
                    next(gens[t])
                except StopIteration:
                    done[t] = True
        it += 1
        if it % 37 == 0:
            if n_aliss[0] - n_alred[0] >= 2:
                alpha_reduce()
            if n_aliss[0] < 16:
                alpha_issue()
    while n_alred[0] < 16:
        if n_aliss[0] < 16 and n_aliss[0] - n_alred[0] < 2:
            alpha_issue()
        else:
            alpha_reduce()
    ev_attn_done_ve = [ev_attn_w[(2 * 7 + t, 3)] for t in range(2)]
    ev_attn_all = list(ev_attn_done_ve)
    ev_pa_free = ev_attn_done_ve[1]
    ev_pb_free = ev_attn_done_ve[1]
    ev_pc_free = ev_attn_done_ve[1]
    ev_last_pv = ev_attn_done_ve[1]

    if debug:
        dg("dbg", d_asum, asum[:], [ev_acc])
    ev_am = te(lambda e: e.matmul(PB[0:1, 0:1], onec32[:], asum[:],
                                  start=True, stop=True),
               [ev_acc, ev_onec32] + ev_attn_done_ve)
    ev_at = ve(lambda e: e.tensor_copy(atot[:], PB[0:1, 0:1]), [ev_am])
    ev_ab = te(lambda e: e.matmul(PB[:, 0:1], oner32[:], atot[:],
                                  start=True, stop=True), [ev_at, ev_oner32])
    ev_a1 = ve(lambda e: e.tensor_scalar(al_a[:], PB[:, 0:1],
                                         1.0 / (float(D) * float(D)), 1.0e-8,
                                         ALU.mult, ALU.max), [ev_ab])
    vgap()
    ev_al128 = ve(lambda e: e.reciprocal(al_r[:], al_a[:]), [ev_a1])
    ev_pb_free = ev_a1

    ev_wslab_rd = [None, None]
    ev_x2_rd_prev = None
    ev_evs_rd = None
    ev_pc2_rd = ev_pc_free
    ev_xq_free = None
    ev_wfq_rd = [None, None]
    ev_pab_free = [ev_pa_free, ev_pb_free]

    ev_x2_store = []
    for half in range(2):
        ev_xo = []
        for t8 in range(8):
            tile = half * 8 + t8
            w = list(ev_x2_store) + (ev_attn_all if half == 0 else [])
            src = xo[tile * 128:(tile + 1) * 128, :]
            dstp = x2f[:, t8 * 2048:(t8 + 1) * 2048]
            if t8 < 4:
                ev_xo.append(dg("lp1", dstp, src, w))
            elif t8 < 6:
                ev_xo.append(dsc("lp0", dstp, src, w))
            else:
                ev_xo.append(ds("lxp", dstp, src, w))
        pab = [PA, PB]
        ev_x2add = None
        for o in range(16):
            sb = o % 2
            ev_sl = dg(f"lw{sb}", wqs[sb][:], wp_t[o * 128:(o + 1) * 128, :],
                       [ev_wslab_rd[sb] if not (half == 0 and o < 2) else ev_qslab_rd[sb]])
            last = None
            for d in range(16):
                w = []
                if d == 0:
                    w = [ev_sl, ev_pab_free[sb]]
                    w += ev_attn_all if (half == 0 and o == 0) else []
                for (r0, r1) in ((0, 512), (512, 1024)):
                    last = te(lambda e, d=d, sb=sb, o=o, r0=r0, r1=r1, half=half:
                              e.matmul(
                        pab[sb][:, r0:r1], wqs[sb][:, d * 128:(d + 1) * 128],
                        qa[:, d * OWN + half * 1024 + r0:d * OWN + half * 1024 + r1],
                        start=(d == 0), stop=(d == 15), skip_group_check=True),
                        w if (r0 == 0 and d == 0) else [])
            ev_wslab_rd[sb] = last
            ev_evac = se(lambda e, sb=sb: e.activation(
                evsb[:, 0:1024], pab[sb][:], ACT.Copy), [last, ev_evs_rd])
            ev_pab_free[sb] = ev_evac
            last_t = None
            for i in range(8):
                w = [ev_evac] if i > 0 else [ev_evac, ev_pc2_rd,
                                             ev_idf if half == 0 and o == 0 else None]
                last_t = te(lambda e, i=i: e.transpose(
                    PC[:, i * 128:(i + 1) * 128],
                    evsb[:, i * 128:(i + 1) * 128], idf[:]), w)
            ev_evs_rd = last_t
            x2ap = x2f.rearrange("p (t c) -> p t c", c=2048)[:, :, o * 128:(o + 1) * 128]
            ev_x2add = ve(lambda e, x2ap=x2ap: e.tensor_tensor(
                x2ap, PC[:].rearrange("p (t c) -> p t c", c=128), x2ap, ALU.add),
                [last_t] + (ev_xo if o == 0 else []))
            ev_pc2_rd = ev_x2add
            ev_x2_last_w = ev_x2add

        ev_dbg_x2 = None
        if debug and half == 0:
            e1 = dg("dbg", d_attn, qa[:, :], ev_attn_all)
            e2 = dg("dbg", d_al, al_a[:], [ev_al128, e1])
            e2 = dg("dbg", d_alr, al_r[:], [ev_al128, e2])
            ev_dbg_x2 = dg("dbg", d_x2a, x2f, [ev_x2add, e2])
        ev_xq_tile = []
        ev_ts3_rd = None
        ev_stage_free = [None, None]
        for t8 in range(8):
            buf = t8 % 2
            stg = evsb[:, 0:2048] if buf == 0 else xbuf[:, 0:2048]
            src = x2f[:, t8 * 2048:(t8 + 1) * 2048]
            ev_ln = emit_ln(buf, src, ev_x2add, stg,
                            [ev_lnrd[buf], ev_evs_rd if t8 == 0 else None,
                             ev_stage_free[buf],
                             (ev_albuf_rd[0] if half == 0 and t8 == 1 else None),
                             (ev_albuf_rd[1] if half == 0 and t8 == 1 else None)])
            ev_lnrd[buf] = ev_ln
            vgap()
            ev_bm = ve(lambda e, stg=stg, buf=buf: e.reduce_max(
                bmax2[buf][:], stg, AX, apply_absolute_value=True), [ev_ln])
            vgap()
            ev_be = ve(lambda e, t8=t8, buf=buf: e.tensor_scalar(
                beta8[:, t8:t8 + 1], bmax2[buf][:], 1.0 / 127.0, 1.0e-8,
                ALU.mult, ALU.max), [ev_bm])
            vgap()
            ev_br = ve(lambda e, t8=t8: e.reciprocal(betr8[:, t8:t8 + 1],
                                                     beta8[:, t8:t8 + 1]), [ev_be])
            vgap()
            q1 = ve(lambda e, t8=t8, stg=stg: e.tensor_scalar(
                stg, stg, betr8[:, t8:t8 + 1], MAGIC,
                ALU.mult, ALU.add), [ev_br])
            vgap()
            q2 = ve(lambda e, stg=stg: e.tensor_scalar(
                stg, stg, MAGIC, 127.0,
                ALU.subtract, ALU.min), [q1])
            vgap()
            q3 = ve(lambda e, stg=stg, buf=buf: e.tensor_scalar(
                wqs[buf][:, 0:2048], stg, -127.0, None,
                ALU.max), [q2, ev_ts3_rd,
                           ev_wslab_rd[buf] if t8 < 2 else None])
            for rnd in range(2):
                pb = ptb_next()
                last_t = None
                for i in range(8):
                    w = [q3, ev_ptbf[pb]] if i == 0 else []
                    if t8 == 0 and rnd == 0 and i == 0:
                        w.append(ev_last_pv if half == 0 else ev_xq_free)
                    last_t = te(lambda e, i=i, rnd=rnd, pb=pb, buf=buf: e.transpose(
                        ptbs[pb][:, i * 128:(i + 1) * 128],
                        wqs[buf][:, (rnd * 8 + i) * 128:(rnd * 8 + i + 1) * 128],
                        idb[:]), w)
                dst_ap = a3[:, rnd * 8192:(rnd + 1) * 8192] \
                    .rearrange("p (i c) -> p i c", c=1024)[:, :, t8 * 128:(t8 + 1) * 128]
                ev_ptbf[pb] = se(lambda e, dst=dst_ap, pb=pb: e.activation(
                    dst, ptbs[pb][:].rearrange("p (i c) -> p i c", c=128),
                    ACT.Copy), [last_t])
                ev_ts3_rd = ev_ptbf[pb]
            ev_stage_free[buf] = ev_ts3_rd
            ev_xq_tile.append(ev_ts3_rd)
        ev_ffn_done_tile = [list() for _ in range(8)]
        ev_q3 = {}

        def emit_wquant(o):
            buf = o % 2
            sbq = o % 2
            ev_wl = dg(f"lg{buf}", xbuf[:, buf * D:(buf + 1) * D],
                       wf_t[o * 128:(o + 1) * 128, :],
                       [ev_albuf_rd[buf] if half == 0 and o < 2 else None,
                        ev_wfq_rd[buf],
                        ev_stage_free[1] if o < 2 else None])
            wlast = None
            for hc in range(2):
                w1 = ve(lambda e, buf=buf, hc=hc: e.tensor_scalar(
                    evsb[:, 1024:2048],
                    xbuf[:, buf * D + hc * 1024:buf * D + (hc + 1) * 1024],
                    al_r[:], 1.25, ALU.mult, ALU.min),
                    [ev_wl, ev_al128,
                     ev_ts3_rd if (o == 0 and hc == 0) else None, wlast])
                vgap()
                w2 = ve(lambda e: e.tensor_scalar(
                    evsb[:, 1024:2048], evsb[:, 1024:2048], -1.25, MAGIC,
                    ALU.max, ALU.add), [w1])
                vgap()
                wlast = ve(lambda e, sbq=sbq, hc=hc: e.tensor_scalar(
                    wqs[sbq][:, hc * 1024:(hc + 1) * 1024], evsb[:, 1024:2048],
                    MAGIC, None, ALU.subtract),
                    [w2, (ev_wslab_rd[sbq] if o >= 2 else ev_stage_free[sbq])
                     if hc == 0 else None])
            ev_wfq_rd[buf] = wlast
            ev_q3[o] = wlast

        emit_wquant(0)
        for o in range(16):
            if o + 1 < 16:
                emit_wquant(o + 1)
            sbq = o % 2
            w3 = ev_q3[o]
            if debug and half == 0 and o == 0:
                w3 = dg("dbg", d_wq, wqs[sbq][:, 0:2048], [w3])
            last = None
            for d in range(16):
                w = [w3, ev_pab_free[o % 2]] + (ev_xq_tile if o == 0 and d == 0 else [])
                for (r0, r1) in ((0, 512), (512, 1024)):
                    last = te(lambda e, d=d, o=o, r0=r0, r1=r1, sbq=sbq: e.matmul(
                        pab[o % 2][:, r0:r1], wqs[sbq][:, d * 128:(d + 1) * 128],
                        a3[:, d * 1024 + r0:d * 1024 + r1],
                        start=(d == 0), stop=(d == 15), skip_group_check=True),
                        w if (d == 0 and r0 == 0) else [])
            ev_wslab_rd[sbq] = last
            ev_evac = se(lambda e, o=o: e.activation(
                evsb[:, 0:1024], pab[o % 2][:], ACT.Copy, scale=al_a[:]),
                [last, ev_evs_rd, ev_al128])
            ev_pab_free[o % 2] = ev_evac
            if debug and half == 0 and o == 0:
                ev_evac = dg("dbg", d_ev, evsb[:, 0:1024], [ev_evac])
            last_t = None
            for i in range(8):
                w = [ev_evac] if i > 0 else [ev_evac, ev_pc2_rd]
                last_t = te(lambda e, i=i: e.transpose(
                    PC[:, i * 128:(i + 1) * 128], evsb[:, i * 128:(i + 1) * 128],
                    idf[:]), w)
            ev_evs_rd = last_t
            for t8 in range(8):
                evb = ve(lambda e, t8=t8, o=o: e.scalar_tensor_tensor(
                    x2f[:, t8 * 2048 + o * 128:t8 * 2048 + (o + 1) * 128],
                    PC[:, t8 * 128:(t8 + 1) * 128], beta8[:, t8:t8 + 1],
                    x2f[:, t8 * 2048 + o * 128:t8 * 2048 + (o + 1) * 128],
                    ALU.mult, ALU.add),
                    [last_t] + ([ev_dbg_x2] if debug and half == 0 and o == 0 and t8 == 0 else []))
                ev_ffn_done_tile[t8].append(evb)
            ev_pc2_rd = ev_ffn_done_tile[7][-1]
        evA = evB = None
        for t8 in range(8):
            tile = half * 8 + t8
            w = [ev_ffn_done_tile[t8][-1]]
            if t8 % 2 == 0:
                evA = dst(y[tile * 128:(tile + 1) * 128, :],
                          x2f[:, t8 * 2048:(t8 + 1) * 2048], w)
            else:
                evB = dsc("sts2", y[tile * 128:(tile + 1) * 128, :],
                          x2f[:, t8 * 2048:(t8 + 1) * 2048], w)
        ev_x2_store = [evA, evB]
        ev_xq_free = last

    total_sts = sch.cnt["sts"]
    total_sts2 = sch.cnt["sts2"]

    semnums = sorted(sems[k].num for k in sems)
    assert semnums[-1] - semnums[0] == len(semnums) - 1, semnums
    nc.gpsimd.sem_clear(range(semnums[0], semnums[-1] + 1))
    nc.all_engine_barrier()
    with nc.allow_low_precision(reason="audited bf16/fp32 mixed pipeline"):
        with nc.Block() as block:
            @block.sync
            def _(sy):
                sch.emit("sync", sy, sems, set())
                sy.wait_ge(sems["sts"], total_sts)
                sy.wait_ge(sems["sts2"], total_sts2)

            @block.gpsimd
            def _(g):
                sch.emit("gpsimd", g, sems, set())

            @block.tensor
            def _(t):
                sch.emit("tensor", t, sems, {"te"})

            @block.vector
            def _(v):
                sch.emit("vector", v, sems, {"ve"})

            @block.scalar
            def _(s):
                sch.emit("scalar", s, sems, {"se"})

    ctx.pop_all()
    return nc


_CACHE = {}


def _tables():
    if "t" in _CACHE:
        return _CACHE["t"]
    half = 64
    inv = 1.0 / (10000.0 ** (np.arange(half, dtype=np.float64) / half))
    pos = np.arange(T, dtype=np.float64)
    ang = pos[:, None] * inv[None, :]
    cos = np.cos(ang)
    sin = np.sin(ang)
    ropek_flat = np.concatenate([cos, -sin, sin], axis=1).astype(np.float32)
    ropek = np.ascontiguousarray(
        ropek_flat.reshape(NKT, 128, 192).transpose(1, 0, 2).reshape(128, NKT * 192))
    rq_cc = {}
    rq_ss = {}
    rows = {}
    for p in range(2):
        rp = np.concatenate([np.arange(g * 128, (g + 1) * 128)
                             for g in range(p, NKT, 2)])
        rows[p] = rp
        c = cos[rp].T
        s = sin[rp].T
        rq_cc[p] = (np.concatenate([c, c], axis=0) * SCALE).astype(np.float32)
        rq_ss[p] = (np.concatenate([-s, s], axis=0) * SCALE).astype(np.float32)
    tri = np.where(np.arange(128)[:, None] > np.arange(128)[None, :],
                   np.float32(NEG), np.float32(0.0))
    full = np.full((128, 128), np.float32(NEG))
    zero = np.zeros((128, 128), np.float32)
    dmask = {}
    for p in range(2):
        slabs = []
        for j in range(NOT_):
            if p == 0:
                slabs.append(np.concatenate([tri, full], axis=1))
            else:
                slabs.append(np.concatenate([zero, tri], axis=1))
        dmask[p] = np.ascontiguousarray(
            np.concatenate(slabs, axis=0).reshape(NOT_, 128, 256)
            .transpose(1, 0, 2).reshape(128, NOT_ * 256))
    _CACHE["t"] = (ropek, rq_cc, rq_ss, dmask, rows)
    return _CACHE["t"]


def blockT(w):
    wt = np.ascontiguousarray(w.T)
    return np.ascontiguousarray(
        wt.reshape(16, 128, 16, 128).transpose(2, 1, 0, 3).reshape(2048, 2048))


def kernel(L0_attn_x, L0_attn_Wq, L0_attn_Wk, L0_attn_Wv, L0_attn_Wp,
           L0_ffn_weight):
    x = np.ascontiguousarray(np.asarray(L0_attn_x, dtype=np.float32))
    wq = np.asarray(L0_attn_Wq, dtype=np.float32)
    wk = np.asarray(L0_attn_Wk, dtype=np.float32)
    wv = np.asarray(L0_attn_Wv, dtype=np.float32)
    wp = np.asarray(L0_attn_Wp, dtype=np.float32)
    wf = np.asarray(L0_ffn_weight, dtype=np.float32)

    wq_t = blockT(wq)
    wp_t = blockT(wp)
    wf_t = blockT(wf)
    wkvT = np.concatenate([wk.T, wv.T], axis=1)
    wkv_t = np.ascontiguousarray(
        wkvT.reshape(16, 128, 1024).transpose(1, 0, 2).reshape(128, 16384))
    ropek, rq_cc, rq_ss, dmask, rows = _tables()
    onescol = np.ones((128, 1), np.float32)
    onesrow = np.ones((1, 128), np.float32)
    identin = np.eye(128, dtype=np.float32)

    if "nc" not in _CACHE:
        _CACHE["nc"] = build_program()
    nc = _CACHE["nc"]

    in_maps = []
    for c in range(8):
        b, p = c // 2, c % 2
        in_maps.append(dict(
            xk=np.ascontiguousarray(x[b]),
            xo=np.ascontiguousarray(x[b][rows[p]]),
            wq_t=wq_t, wkv_t=wkv_t, wp_t=wp_t, wf_t=wf_t,
            ropeq_cc=rq_cc[p], ropeq_ss=rq_ss[p], ropek=ropek,
            dmaskd=dmask[p], onescol=onescol, onesrow=onesrow,
            identin=identin,
        ))
    res = run_bass_kernel_spmd(nc, in_maps, core_ids=list(range(8)))
    out = np.empty((NB, T, D), np.float32)
    for c in range(8):
        b, p = c // 2, c % 2
        out[b][rows[p]] = res.results[c]["y"]
    return out


# version 29 (final)
# speedup vs baseline: 1.0110x; 1.0110x over previous
import numpy as np
import ml_dtypes
from contextlib import ExitStack

import concourse.bass as bass
import concourse.mybir as mybir
from concourse.bass_utils import run_bass_kernel_spmd

F32 = mybir.dt.float32
BF16 = mybir.dt.bfloat16
ALU = mybir.AluOpType
ACT = mybir.ActivationFunctionType
AX = mybir.AxisListType.X

D = 2048
T = 4096
NB = 4
HQ = 16
HKV = 4
DH = 128
NKT = 32
NOT_ = 16
OWN = NOT_ * 128
NEG = -1.0e30
MAGIC = 12582912.0
SCALE = 1.0 / float(np.sqrt(DH))
INVD = 1.0 / D
EPS_LN = 1.0e-5


class Ev:
    __slots__ = ("sem", "val")

    def __init__(self, sem, val):
        self.sem = sem
        self.val = val


class Sched:
    def __init__(self):
        self.ops = []
        self.cnt = {}

    def add(self, eng, fn, waits=(), inc=None, n=1):
        ev = None
        if inc is not None:
            self.cnt[inc] = self.cnt.get(inc, 0) + n
            ev = Ev(inc, self.cnt[inc])
        self.ops.append((eng, fn, tuple(w for w in waits if w is not None), inc, n))
        return ev

    def emit(self, eng_name, eng, sems, own_sems):
        water = {}
        for e, fn, waits, inc, n in self.ops:
            if e != eng_name:
                continue
            for w in waits:
                if w.sem in own_sems:
                    continue
                if water.get(w.sem, -1) >= w.val:
                    continue
                eng.wait_ge(sems[w.sem], w.val)
                water[w.sem] = w.val
            ins = fn(eng)
            if inc is not None:
                ins.then_inc(sems[inc], n)


def build_program(debug=False):
    nc = bass.Bass("TRN2", target_bir_lowering=False, debug=False)

    xk = nc.dram_tensor("xk", [T, D], F32, kind="ExternalInput").ap()
    xo = nc.dram_tensor("xo", [OWN, D], F32, kind="ExternalInput").ap()
    wq_t = nc.dram_tensor("wq_t", [D, D], F32, kind="ExternalInput").ap()
    wkv_t = nc.dram_tensor("wkv_t", [128, 16 * 1024], F32, kind="ExternalInput").ap()
    wp_t = nc.dram_tensor("wp_t", [D, D], F32, kind="ExternalInput").ap()
    wf_t = nc.dram_tensor("wf_t", [D, D], F32, kind="ExternalInput").ap()
    ropeq_cc = nc.dram_tensor("ropeq_cc", [128, OWN], F32, kind="ExternalInput").ap()
    ropeq_ss = nc.dram_tensor("ropeq_ss", [128, OWN], F32, kind="ExternalInput").ap()
    ropek = nc.dram_tensor("ropek", [128, NKT * 192], F32, kind="ExternalInput").ap()
    dmaskd = nc.dram_tensor("dmaskd", [128, NOT_ * 256], F32, kind="ExternalInput").ap()
    onescol = nc.dram_tensor("onescol", [128, 1], F32, kind="ExternalInput").ap()
    onesrow = nc.dram_tensor("onesrow", [1, 128], F32, kind="ExternalInput").ap()
    identin = nc.dram_tensor("identin", [128, 128], F32, kind="ExternalInput").ap()
    y = nc.dram_tensor("y", [OWN, D], F32, kind="ExternalOutput").ap()
    if debug:
        d_kf = nc.dram_tensor("d_kf", [128, HKV * T], F32, kind="ExternalOutput").ap()
        d_vt = nc.dram_tensor("d_vt", [128, NKT * 512], F32, kind="ExternalOutput").ap()
        d_q = nc.dram_tensor("d_q", [128, HQ * OWN], F32, kind="ExternalOutput").ap()
        d_negb = nc.dram_tensor("d_negb", [128, 4 * HQ], F32, kind="ExternalOutput").ap()
        d_attn = nc.dram_tensor("d_attn", [128, HQ * OWN], F32, kind="ExternalOutput").ap()
        d_x2a = nc.dram_tensor("d_x2a", [128, 16384], F32, kind="ExternalOutput").ap()
        d_al = nc.dram_tensor("d_al", [128, 1], F32, kind="ExternalOutput").ap()
        d_alr = nc.dram_tensor("d_alr", [128, 1], F32, kind="ExternalOutput").ap()
        d_W = nc.dram_tensor("d_W", [128, 24], F32, kind="ExternalOutput").ap()
        d_x1 = nc.dram_tensor("d_x1", [128, 2048], F32, kind="ExternalOutput").ap()
        d_asum = nc.dram_tensor("d_asum", [128, 1], F32, kind="ExternalOutput").ap()
        d_xb = nc.dram_tensor("d_xb", [128, 2048], F32, kind="ExternalOutput").ap()
        d_st = nc.dram_tensor("d_st", [128, 8], F32, kind="ExternalOutput").ap()
        d_xq = nc.dram_tensor("d_xq", [128, 16384], F32, kind="ExternalOutput").ap()
        d_beta = nc.dram_tensor("d_beta", [128, 8], F32, kind="ExternalOutput").ap()
        d_wq = nc.dram_tensor("d_wq", [128, 2048], F32, kind="ExternalOutput").ap()
        d_ev = nc.dram_tensor("d_ev", [128, 1024], F32, kind="ExternalOutput").ap()

    sch = Sched()
    ctx = ExitStack()
    E = ctx.enter_context

    qa = E(nc.sbuf_tensor("qa", [128, HQ * OWN], BF16))
    kf_h = nc.alloc_sbuf_tensor("kf", [128, HKV * T], BF16)
    vt_h = nc.alloc_sbuf_tensor("vt", [128, NKT * 512], BF16)
    kf = kf_h.ap()
    vt = vt_h.ap()
    kf_addr = nc.lookup_mloc(kf_h).addr
    x2f = nc.alloc_sbuf_tensor_at("x2f", [128, 16384], F32, offset=kf_addr).ap()
    a3 = E(nc.sbuf_tensor("a3", [128, 16384], BF16))
    xbuf_h = nc.alloc_sbuf_tensor("xbuf", [128, 2 * D], F32)
    xbuf = xbuf_h.ap()
    xbufb = nc.alloc_sbuf_tensor_at(
        "xbufb", [128, 4 * D], BF16,
        offset=nc.lookup_mloc(xbuf_h).addr).ap()
    wqs = [E(nc.sbuf_tensor("wqs0", [128, 2048], BF16)),
           E(nc.sbuf_tensor("wqs1", [128, 2048], BF16))]
    tab8 = E(nc.sbuf_tensor("tab8", [128, 4096], BF16))
    evsb = E(nc.sbuf_tensor("evsb", [128, 2048], F32))
    rsig = evsb[:, 0:1024]
    idb = E(nc.sbuf_tensor("idb", [128, 128], BF16))
    idf = E(nc.sbuf_tensor("idf", [128, 128], F32))
    onec = E(nc.sbuf_tensor("onec", [128, 1], BF16))
    onec32 = E(nc.sbuf_tensor("onec32", [128, 1], F32))
    oner = E(nc.sbuf_tensor("oner", [1, 128], BF16))
    oner32 = E(nc.sbuf_tensor("oner32", [1, 128], F32))
    epsv = E(nc.sbuf_tensor("epsv", [128, 1], F32))
    st_ln = [E(nc.sbuf_tensor(f"st_ln{i}", [128, 1], F32)) for i in range(2)]
    st_rs = [E(nc.sbuf_tensor(f"st_rs{i}", [128, 1], F32)) for i in range(2)]
    bnst = [E(nc.sbuf_tensor(f"bnst{i}", [128, 24], F32)) for i in range(2)]
    bnagg = [E(nc.sbuf_tensor(f"bnagg{i}", [128, 2], F32)) for i in range(2)]
    negb = E(nc.sbuf_tensor("negb", [128, 4 * HQ], F32))
    qmax64 = E(nc.sbuf_tensor("qmax64", [128, 4 * HQ], F32))
    kmax64 = E(nc.sbuf_tensor("kmax64", [128, 4 * HQ], F32))
    bln64 = E(nc.sbuf_tensor("bln64", [128, 4 * HQ], F32))

    asum = E(nc.sbuf_tensor("asum", [128, 1], F32))
    asum_t = E(nc.sbuf_tensor("asum_t", [128, 1], F32))
    atot = E(nc.sbuf_tensor("atot1", [1, 1], F32))
    al_a = E(nc.sbuf_tensor("al_a", [128, 1], F32))
    al_r = E(nc.sbuf_tensor("al_r", [128, 1], F32))
    bmax2 = [E(nc.sbuf_tensor(f"bmax{i}", [128, 1], F32)) for i in range(2)]
    beta8 = E(nc.sbuf_tensor("beta8", [128, 8], F32))
    dstats = E(nc.sbuf_tensor("dstats", [128, 8], F32))
    gapt = E(nc.sbuf_tensor("gapt", [128, 1], F32))
    gapt_s = E(nc.sbuf_tensor("gapt_s", [128, 1], F32))
    gsrc_s = E(nc.sbuf_tensor("gsrc_s", [128, 1], F32))
    gsrc_v = E(nc.sbuf_tensor("gsrc_v", [128, 1], F32))
    accA = E(nc.sbuf_tensor("accA", [128, 1], F32))
    accB = E(nc.sbuf_tensor("accB", [128, 1], F32))
    betr8 = E(nc.sbuf_tensor("betr8", [128, 8], F32))

    PA = E(nc.psum_tensor("PA", [128, 1024], F32))
    PB = E(nc.psum_tensor("PB", [128, 1024], F32))
    PC = E(nc.psum_tensor("PC", [128, 1024], F32))
    PTB = E(nc.psum_tensor("PTB", [128, 1024], BF16))
    PTB2 = E(nc.psum_tensor("PTB2", [128, 1024], BF16))

    sems = {}
    for s in ["te", "ve", "se", "sts", "sts2", "ldc", "lds", "ldq", "lw0", "lw1",
              "lg0", "lg1", "lx0", "lx1", "lx2", "lx3", "lxp", "lp0", "lp1",
              "dbg"]:
        sems[s] = E(nc.semaphore(s))

    def dg(sem, out, in_, waits=()):
        return sch.add("gpsimd", lambda e, o=out, i=in_: e.dma_start(out=o, in_=i),
                       waits, inc=sem, n=16)

    def ds(sem, out, in_, waits=()):
        return sch.add("sync", lambda e, o=out, i=in_: e.dma_start(out=o, in_=i),
                       waits, inc=sem, n=16)

    def dst(out, in_, waits=()):
        return sch.add("sync", lambda e, o=out, i=in_: e.dma_start(out=o, in_=i),
                       waits, inc="sts", n=16)

    def dsc(sem, out, in_, waits=()):
        return sch.add("scalar", lambda e, o=out, i=in_: e.dma_start(out=o, in_=i),
                       waits, inc=sem, n=16)

    def vgap():
        ve(lambda e: e.tensor_copy(gapt[:], onec32[:]), [ev_onec32])

    def sgap():
        se(lambda e: e.copy(gapt_s[:], onec32[:]), [ev_onec32])

    def te(fn, waits=()):
        return sch.add("tensor", fn, waits, inc="te")

    def ve(fn, waits=()):
        return sch.add("vector", fn, waits, inc="ve")

    def se(fn, waits=()):
        return sch.add("scalar", fn, waits, inc="se")

    ev_idb = dg("ldc", idb[:], identin)
    ev_idf = ds("lds", idf[:], identin)
    ev_onec = dg("ldc", onec[:], onescol)
    ev_onec32 = ds("lds", onec32[:], onescol)
    ev_oner = dg("ldc", oner[:], onesrow)
    ev_oner32 = ds("lds", oner32[:], onesrow)
    ev_wkv = dg("ldc", qa[:, 0:16 * 1024], wkv_t)
    ev_ropek = dg("ldc", a3[:, 8192:8192 + NKT * 192], ropek)
    ev_ldc_all = Ev("ldc", sch.cnt["ldc"])
    ev_lds_all = Ev("lds", sch.cnt["lds"])
    ev_idb = ev_onec = ev_oner = ev_wkv = ev_ropek = ev_ldc_all
    ev_idf = ev_onec32 = ev_oner32 = ev_lds_all
    ev_eps = ve(lambda e: e.memset(epsv[:], EPS_LN))

    from concourse.hw_specs import get_activation_tables

    def _preload(e):
        for _i, (_n, _f) in enumerate(get_activation_tables(nc.m.arch).items()):
            if _n == "natural_log_exp_and_others":
                return e.add_instruction(mybir.InstLoadActFuncSet(
                    name=nc.make_name("acttab", add_next_id=True),
                    act_func_set_id=_i, ins=[], outs=[]))
        raise RuntimeError("no natural_log_exp_and_others table set")
    se(_preload)

    def emit_ln(buf, src_ap, ev_src, out_ap, out_waits=()):
        w = [ev_src, ev_eps]
        e1 = None
        for c in range(4):
            e1 = ve(lambda e, c=c: e.bn_stats(
                bnst[buf][:, 6 * c:6 * (c + 1)],
                src_ap[:, 512 * c:512 * (c + 1)]), w if c == 0 else [])
        vgap()
        e5 = ve(lambda e: e.bn_aggr(bnagg[buf][:], bnst[buf][:]), [e1])
        e6 = se(lambda e: e.activation(st_ln[buf][:], bnagg[buf][:, 1:2], ACT.Ln,
                                       bias=epsv[:]), [e5])
        sgap()
        e7 = se(lambda e: e.activation(st_rs[buf][:], st_ln[buf][:], ACT.Exp,
                                       scale=-0.5), [e6])
        vgap()
        e8 = ve(lambda e: e.tensor_scalar(out_ap, src_ap, bnagg[buf][:, 0:1],
                                          st_rs[buf][:], ALU.subtract, ALU.mult),
                [e7] + list(out_waits))
        return e8

    SQ_A3 = a3[:, 14336:16384]

    pkv = [PA, PB]
    ev_lnrd = [None, None]
    ev_kvmm = [None, None]
    ev_kv_free = [None, None]
    ev_x1tm_free = [None, None]
    ev_x1fm_free = [None, None]
    ptbs = [PTB, PTB2]
    ev_ptbf = [None, None]
    pidx = [0]

    def ptb_next():
        b = pidx[0] % 2
        pidx[0] += 1
        return b
    ev_vt_last = None
    ev_kf_last = None
    ev_p1_te_last = None

    ev_lnx = [None] * 4
    ev_ln_t = {}
    ev_trd = {}
    X1TM_BASE = [0, 2048, 14336]
    ev_x1tm_free3 = [None, None, None]

    def p1_ln(kt):
        slot3 = kt % 3
        s3 = kt % 4
        ev_x = dg(f"lx{s3}", xbufb[:, s3 * D:(s3 + 1) * D],
                  xk[kt * 128:(kt + 1) * 128, :], [ev_lnx[s3]])
        ev_ln = emit_ln(kt % 2, xbufb[:, s3 * D:(s3 + 1) * D], ev_x,
                        a3[:, X1TM_BASE[slot3]:X1TM_BASE[slot3] + 2048],
                        [ev_x1tm_free3[slot3]])
        ev_lnx[s3] = ev_ln
        ev_lnrd[kt % 2] = ev_ln
        ev_ln_t[kt] = ev_ln

    def p1_tr(kt):
        buf = kt % 2
        base = X1TM_BASE[kt % 3]
        for rnd in range(2):
            pb = ptb_next()
            last_t = None
            for i in range(8):
                w = [ev_ln_t[kt], ev_ptbf[pb]] if i == 0 else []
                last_t = te(lambda e, i=i, rnd=rnd, b=base, pb=pb: e.transpose(
                    ptbs[pb][:, i * 128:(i + 1) * 128],
                    a3[:, b + (rnd * 8 + i) * 128:b + (rnd * 8 + i + 1) * 128],
                    idb[:]), w + ([ev_idb] if kt == 0 and rnd == 0 else []))
            ev_ptbf[pb] = se(lambda e, rnd=rnd, buf=buf, pb=pb: e.activation(
                a3[:, 4096 + buf * 2048 + rnd * 1024:
                   4096 + buf * 2048 + (rnd + 1) * 1024], ptbs[pb][:], ACT.Copy),
                [last_t, ev_x1fm_free[buf] if rnd == 0 else None])
        ev_trd[kt] = ev_ptbf[(pidx[0] - 1) % 2]

    ev_m3_t = {}

    def k_tr(kt):
        nonlocal ev_p1_te_last, ev_kf_last
        base = X1TM_BASE[kt % 3]
        kh = a3[:, base + 1024:base + 1536]
        pb = ptb_next()
        last_t = None
        for i in range(4):
            w = [ev_m3_t[kt], ev_ptbf[pb]] if i == 0 else []
            last_t = te(lambda e, i=i, kh=kh, pb=pb: e.transpose(
                ptbs[pb][:, i * 128:(i + 1) * 128], kh[:, i * 128:(i + 1) * 128],
                idb[:]), w)
        ev_x1tm_free3[kt % 3] = last_t
        ev_p1_te_last = last_t
        kfdst = kf.rearrange("p (hk r) -> p hk r", hk=HKV)[:, :, kt * 128:(kt + 1) * 128]
        ev_ptbf[pb] = se(lambda e, kfdst=kfdst, pb=pb: e.activation(
            kfdst, ptbs[pb][:, 0:512].rearrange("p (hk r) -> p hk r", hk=4),
            ACT.Copy), [last_t])
        ev_kf_last = ev_ptbf[pb]

    p1_ln(0)
    p1_ln(1)
    p1_tr(0)
    for kt in range(NKT):
        if kt + 1 < NKT:
            p1_tr(kt + 1)
        if kt >= 1:
            k_tr(kt - 1)
        if kt + 2 < NKT:
            p1_ln(kt + 2)
        buf = kt % 2
        base = X1TM_BASE[kt % 3]
        ev_x1fm = ev_trd[kt]
        for d in range(16):
            x1fm_d = a3[:, 4096 + buf * 2048 + d * 128:4096 + buf * 2048 + (d + 1) * 128]
            w = [ev_x1fm, ev_wkv, ev_kv_free[buf]] if d == 0 else []
            te(lambda e, d=d, x1=x1fm_d, P=pkv[buf]: e.matmul(
                P[:, 0:512], x1, qa[:, d * 1024:d * 1024 + 512],
                start=(d == 0), stop=(d == 15), skip_group_check=True), w)
            ev_kvmm[buf] = te(lambda e, d=d, x1=x1fm_d, P=pkv[buf]: e.matmul(
                P[:, 512:1024], x1, qa[:, d * 1024 + 512:(d + 1) * 1024],
                start=(d == 0), stop=(d == 15), skip_group_check=True), [])
        ev_vt_last = se(lambda e, kt=kt, P=pkv[buf]: e.activation(
            vt[:, kt * 512:(kt + 1) * 512], P[:, 512:1024], ACT.Copy),
            [ev_kvmm[buf]])
        ka = a3[:, base + 0:base + 512]
        kh = a3[:, base + 1024:base + 1536]
        ksw = a3[:, base + 1536:base + 2048]
        kev = pkv[buf][:, 0:512]
        r4 = lambda ap: ap.rearrange("p (h two f) -> p h two f", h=4, two=2)
        sw1 = ve(lambda e, ksw=ksw, kev=kev: e.tensor_copy(
            r4(ksw)[:, :, 0, :], r4(kev)[:, :, 1, :]), [ev_kvmm[buf]])
        ve(lambda e, ksw=ksw, kev=kev: e.tensor_copy(
            r4(ksw)[:, :, 1, :], r4(kev)[:, :, 0, :]), [])
        kb_t = kt * 192 + 8192
        tcc = a3[:, kb_t:kb_t + 64].rearrange("p (a b f) -> p a b f", a=1, b=1) \
            .broadcast_to([128, 4, 2, 64])
        tss = a3[:, kb_t + 64:kb_t + 192].rearrange("p (a two f) -> p a two f",
                                                    a=1, two=2) \
            .broadcast_to([128, 4, 2, 64])
        ev_m1 = ve(lambda e, ka=ka, kev=kev, tcc=tcc: e.tensor_tensor(
            r4(ka), r4(kev), tcc, ALU.mult), [ev_ropek] if kt == 0 else [])
        ev_kv_free[buf] = ev_m1
        ev_m2 = ve(lambda e, ksw=ksw, tss=tss: e.tensor_tensor(
            r4(ksw), r4(ksw), tss, ALU.mult), [])
        vgap()
        ev_m3_t[kt] = ve(lambda e, kh=kh, ka=ka, ksw=ksw: e.tensor_tensor(
            kh[:], ka[:], ksw[:], ALU.add), [ev_m2])
        ev_x1fm_free[buf] = ev_kvmm[buf]
    k_tr(NKT - 1)
    ev_x1tm_free = [ev_x1tm_free3[0], ev_x1tm_free3[1]]

    ev_wkv_rd = ev_kvmm[(NKT - 1) % 2]
    ev_dbg_kv = None
    if debug:
        e1 = dg("dbg", d_kf, kf, [ev_kf_last, ev_vt_last])
        ev_dbg_kv = dg("dbg", d_vt, vt, [e1])

    ev_dbg_x1 = None
    if debug:
        ev_dbg_x1 = dg("dbg", d_x1, a3[:, 2048:4096], [ev_lnrd[1]])

    krow = E(nc.sbuf_tensor("krow", [128, 32], F32))
    lnrow = [E(nc.sbuf_tensor(f"lnrow{i}", [1, 512], F32)) for i in range(2)]
    kmax128 = [E(nc.sbuf_tensor(f"kmax128_{i}", [128, 1], F32)) for i in range(HKV)]
    dbgW = E(nc.sbuf_tensor("dbgW", [128, 24], F32))
    nsq_ring = [a3[:, 14336 + i * 512:14336 + (i + 1) * 512] for i in range(4)]
    ev_ring_mm = [None] * 4
    ev_row_ps = None
    ev_bank1_free = None
    ev_sq_u = {}
    nunits = []
    for g_ in range(16):
        nunits.append(('q', g_, 0))
        nunits.append(('k', 2 * g_ // 8, (2 * g_) % 8))
        nunits.append(('k', (2 * g_ + 1) // 8, (2 * g_ + 1) % 8))
    for qq_ in range(1, 4):
        for h_ in range(16):
            nunits.append(('q', h_, qq_))
    n_sq = [0]
    n_ch = [0]

    def emit_sq(extra=(), src=None, ascale=1.0):
        u = n_sq[0]
        slot = u % 4
        kind, a, b = nunits[u]
        if src is None:
            src = kf[:, a * T + b * 512:(a * T + (b + 1) * 512)] if kind == 'k' \
                else qa[:, a * OWN + b * 512:a * OWN + (b + 1) * 512]
        w = [ev_ring_mm[slot]] + list(extra)
        if kind == 'k':
            w.append(ev_kf_last)
        if u < 4:
            w.append(ev_x1tm_free3[2])
        ev_sq_u[u] = se(lambda e, slot=slot, src=src, ascale=ascale: e.activation(
            nsq_ring[slot], src, ACT.Square, scale=ascale), w)
        n_sq[0] = u + 1
        return ev_sq_u[u]

    def emit_chain():
        nonlocal ev_row_ps, ev_bank1_free
        u = n_ch[0]
        slot = u % 4
        ev_mm = te(lambda e, slot=slot: e.matmul(
            PC[0:1, 0:512], onec[:], nsq_ring[slot],
            start=True, stop=True), [ev_sq_u.pop(u), ev_onec, ev_row_ps])
        ev_cp = ve(lambda e, slot=slot: e.tensor_copy(
            nsq_ring[slot][0:1, :], PC[0:1, 0:512]), [ev_mm])
        ev_row_ps = ev_cp
        ev_bc = te(lambda e, slot=slot: e.matmul(
            PC[:, 512:1024], oner[:], nsq_ring[slot][0:1, :],
            start=True, stop=True), [ev_cp, ev_oner, ev_bank1_free])
        ev_ring_mm[slot] = ev_bc
        kind, a, b = nunits[u]
        dst = krow[:, 8 * a + b:8 * a + b + 1] if kind == 'k' \
            else qmax64[:, 4 * a + b:4 * a + b + 1]
        ev_bank1_free = ve(lambda e, dst=dst: e.reduce_max(
            dst, PC[:, 512:1024], AX), [ev_bc])
        n_ch[0] = u + 1

    def catch_chains(target):
        while n_ch[0] < min(target, n_sq[0]):
            emit_chain()

    ev_qslab = [None, None]
    ev_qslab_rd = [None, None]
    ev_qpsum_free = [None] * 4
    ev_qtmp_free = [None, None]
    ev_rqcc = dg("ldq", tab8[:, 0:2048], ropeq_cc, [])
    ev_rqss = dg("ldq", tab8[:, 2048:4096], ropeq_ss, [])
    ev_qa_w = {}
    ev_x1qfm_free = None
    ev_last_qmm = None
    qslot = [(PA, 0), (PA, 512), (PB, 0), (PB, 512)]

    ev_p2ln = {}
    ev_x1tm_rd = [None, None]

    def p2_ln(tile, extra=()):
        buf = tile % 2
        s3 = tile % 4
        ev_x = dg(f"lx{s3}", xbufb[:, s3 * D:(s3 + 1) * D],
                  xo[tile * 128:(tile + 1) * 128, :], [ev_lnx[s3]])
        ev_ln = emit_ln(buf, xbufb[:, s3 * D:(s3 + 1) * D], ev_x,
                        a3[:, buf * 2048:buf * 2048 + 2048],
                        [ev_x1tm_rd[buf]] + list(extra))
        ev_lnx[s3] = ev_ln
        ev_lnrd[buf] = ev_ln
        ev_p2ln[tile] = ev_ln

    for qq in range(4):
        ev_tile_fm = None
        for tq in range(4):
            tile = qq * 4 + tq
            buf = tile % 2
            x1tm_base = buf * 2048
            if tile not in ev_p2ln:
                p2_ln(tile, [ev_x1tm_free[buf], ev_dbg_x1]
                      if qq == 0 and tq < 2 else [])
            ev_ln = ev_p2ln[tile]
            last_tr = None
            for rnd in range(2):
                pb = ptb_next()
                last_t = None
                for i in range(8):
                    w = [ev_ln, ev_ptbf[pb]] if i == 0 else []
                    if qq == 0 and tq == 0 and rnd == 0 and i == 0:
                        w.append(ev_p1_te_last)
                    last_t = te(lambda e, i=i, rnd=rnd, b=x1tm_base, pb=pb:
                                e.transpose(
                        ptbs[pb][:, i * 128:(i + 1) * 128],
                        a3[:, b + (rnd * 8 + i) * 128:b + (rnd * 8 + i + 1) * 128],
                        idb[:]), w)
                last_tr = last_t
                dst_ap = a3[:, 4096 + rnd * 4096:4096 + (rnd + 1) * 4096] \
                    .rearrange("p (i c) -> p i c", c=512)[:, :, tq * 128:(tq + 1) * 128]
                wv = [last_t]
                if qq > 0 and tq == 0 and rnd == 0:
                    wv.append(ev_x1qfm_free)
                ev_tile_fm = se(lambda e, dst=dst_ap, pb=pb: e.activation(
                    dst, ptbs[pb][:].rearrange("p (i c) -> p i c", c=128),
                    ACT.Copy), wv)
                ev_ptbf[pb] = ev_tile_fm
            ev_x1tm_rd[buf] = last_tr
        for h in range(HQ):
            sb = h % 2
            ev_qslab[sb] = dg(
                f"lw{sb}", wqs[sb][:], wq_t[h * 128:(h + 1) * 128, :],
                [ev_qslab_rd[sb]])
            P, off = qslot[h % 4]
            last = None
            for d in range(16):
                w = [ev_qslab[sb], ev_tile_fm, ev_qpsum_free[h % 4]] if d == 0 else []
                last = te(lambda e, d=d, P=P, off=off, sb=sb: e.matmul(
                    P[:, off:off + 512], wqs[sb][:, d * 128:(d + 1) * 128],
                    a3[:, 4096 + d * 512:4096 + (d + 1) * 512],
                    start=(d == 0), stop=(d == 15), skip_group_check=True), w)
            ev_qslab_rd[sb] = last
            ev_last_qmm = last
            qb = h % 2
            qt_base = 12288 + qb * 1024
            qtmp = a3[:, qt_base:qt_base + 512]
            qsw = a3[:, qt_base + 512:qt_base + 1024]
            ev_ev = se(lambda e, qtmp=qtmp, P=P, off=off: e.activation(
                qtmp, P[:, off:off + 512], ACT.Copy), [last, ev_qtmp_free[qb]])
            ev_qpsum_free[h % 4] = emit_sq([last], src=P[:, off:off + 512],
                                           ascale=SCALE)
            if qq == 0:
                emit_sq()
                emit_sq()
            catch_chains(n_sq[0] - 3)
            if h >= 14 and qq < 3:
                p2_ln(4 * (qq + 1) + (h - 14))
            s1 = ve(lambda e, qsw=qsw, qtmp=qtmp: e.tensor_copy(
                qsw[0:64, :], qtmp[64:128, :]), [ev_ev, ev_rqcc, ev_rqss])
            ve(lambda e, qsw=qsw, qtmp=qtmp: e.tensor_copy(
                qsw[64:128, :], qtmp[0:64, :]), [])
            ve(lambda e, qtmp=qtmp, qq=qq: e.tensor_tensor(
                qtmp[:], qtmp[:], tab8[:, qq * 512:(qq + 1) * 512], ALU.mult), [])
            ve(lambda e, qsw=qsw, qq=qq: e.tensor_tensor(
                qsw[:], qsw[:], tab8[:, 2048 + qq * 512:2048 + (qq + 1) * 512],
                ALU.mult), [])
            vgap()
            ev_rope = ve(lambda e, h=h, qq=qq, qtmp=qtmp, qsw=qsw: e.tensor_tensor(
                qa[:, h * OWN + qq * 512:h * OWN + (qq + 1) * 512],
                qtmp[:], qsw[:], ALU.add), [ev_wkv_rd])
            ev_qtmp_free[qb] = ev_rope
            ev_qa_w[(h, qq)] = ev_rope
        ev_x1qfm_free = ev_last_qmm

    catch_chains(96)
    ev_km = []
    for hk in range(HKV):
        ev_km.append(ve(lambda e, hk=hk: e.reduce_max(
            kmax128[hk][:], krow[:, 8 * hk:8 * (hk + 1)], AX), []))
    vgap()
    ev_kx = []
    for hk in range(HKV):
        ev_kx.append(ve(lambda e, hk=hk: e.tensor_copy(
            kmax64[:, 16 * hk:16 * (hk + 1)],
            kmax128[hk][:].broadcast_to([128, 16])), [ev_km[hk]]))
        ve(lambda e, hk=hk: e.tensor_copy(
            dbgW[:, 16 + hk:17 + hk], kmax128[hk][:]), [])
    vgap()
    eb1 = ve(lambda e: e.tensor_tensor(bln64[:], qmax64[:], kmax64[:], ALU.mult),
             ev_kx)
    eb2 = se(lambda e: e.activation(bln64[:], bln64[:], ACT.Ln), [eb1])
    sgap()
    eb3 = se(lambda e: e.activation(negb[:], bln64[:], ACT.Exp, scale=0.5), [eb2])
    ev_negb = ve(lambda e: e.tensor_scalar(negb[:], negb[:], -1.0, 38.0,
                                           ALU.mult, ALU.add), [eb3])

    ev_dbg_q = None
    if debug:
        e1 = dg("dbg", d_q, qa[:, :], [ev_qa_w[(HQ - 1, 3)]])
        e1 = dg("dbg", d_W, dbgW[:], [ev_negb, e1])
        ev_dbg_q = dg("dbg", d_negb, negb[:], [e1])

    ev_albuf_rd = [None, None]
    ev_acc = None
    ev_alw = [None] * 16
    n_aliss = [0]
    n_alred = [0]

    def alpha_issue():
        sl = n_aliss[0]
        n_aliss[0] = sl + 1
        buf = sl % 2
        ev_alw[sl] = dg(f"lg{buf}", xbuf[:, buf * D:(buf + 1) * D],
                        wf_t[sl * 128:(sl + 1) * 128, :],
                        [ev_albuf_rd[buf]] if sl >= 2 else list(ev_lnx))

    def alpha_reduce():
        sl = n_alred[0]
        n_alred[0] = sl + 1
        nonlocal ev_acc
        buf = sl % 2
        ev_r = ve(lambda e, buf=buf: e.reduce_sum(
            asum_t[:], xbuf[:, buf * D:(buf + 1) * D], AX,
            apply_absolute_value=True), [ev_alw[sl]])
        ev_albuf_rd[buf] = ev_r
        vgap()
        if sl == 0:
            ev_acc = ve(lambda e: e.tensor_copy(accA[:], asum_t[:]), [ev_r])
        elif sl % 2 == 1:
            dst_ = asum[:] if sl == 15 else accB[:]
            ev_acc = ve(lambda e, dst_=dst_: e.tensor_tensor(
                dst_, accA[:], asum_t[:], ALU.add), [ev_r, ev_acc])
        else:
            ev_acc = ve(lambda e: e.tensor_tensor(accA[:], accB[:], asum_t[:],
                                                  ALU.add), [ev_r, ev_acc])

    ev_dm = dg("ldq", tab8[:, 0:4096], dmaskd,
               [ev_qa_w[(HQ - 1, 3)], ev_rqcc, ev_rqss])
    ev_entry = [ev_kf_last, ev_negb, ev_dm, ev_qpsum_free[0], ev_qpsum_free[1],
                ev_qpsum_free[2], ev_qpsum_free[3], ev_row_ps, ev_bank1_free,
                ev_onec, ev_oner]

    def stream_gen(t):
        pv = PA[:, t * 512:(t + 1) * 512]
        stp = PB[:, t * 512:(t + 1) * 512]
        sg = PC[0:1, t * 512:(t + 1) * 512]
        sgrow = a3[0:1, 3072 + t * 512:3072 + (t + 1) * 512]
        tln = evsb[:, 1024 + t * 512:1024 + (t + 1) * 512]
        rsg = evsb[:, t * 512:(t + 1) * 512]
        ev_st_free = None
        ev_pv_free = None
        ev_sg_free = None
        ev_ptfree = [None, None]
        first = True
        for hh in range(8):
            for s4 in range(4):
                h = 2 * hh + t
                hk = h // 4
                ktmax = 8 * (s4 + 1)
                ev_exp = None
                ev_pend = None
                ev_pv_last = None
                ev_sg_last = None
                for kt in range(ktmax):
                    jm = kt // 2
                    c0 = max(jm * 128 - s4 * 512, 0)
                    W = 512 - c0
                    w = [ev_exp if kt > 0 else ev_st_free]
                    if first and kt == 0:
                        w += ev_entry
                    ev_st = te(lambda e, h=h, hk=hk, kt=kt, s4=s4, c0=c0, stp=stp:
                               e.matmul(
                        stp[:, c0:512],
                        kf[:, hk * T + kt * 128:hk * T + (kt + 1) * 128],
                        qa[:, h * OWN + s4 * 512 + c0:h * OWN + (s4 + 1) * 512],
                        start=True, stop=True, skip_group_check=True), w)
                    ev_msk = None
                    if kt >= 8 * s4:
                        mc = jm * 128 - s4 * 512
                        dmc = jm * 256 + (kt - 2 * jm) * 128
                        ev_msk = ve(lambda e, mc=mc, dmc=dmc, stp=stp:
                                    e.tensor_tensor(
                            stp[:, mc:mc + 128], stp[:, mc:mc + 128],
                            tab8[:, dmc:dmc + 128], ALU.add), [ev_st])
                    ps = kt % 2
                    ptap = a3[:, t * 1024 + ps * 512:t * 1024 + ps * 512 + W]
                    ev_exp = se(lambda e, ptap=ptap, c0=c0, h=h, s4=s4, stp=stp:
                                e.activation(
                        ptap, stp[:, c0:512], ACT.Exp,
                        bias=negb[:, 4 * h + s4:4 * h + s4 + 1]),
                        [ev_msk if ev_msk is not None else ev_st, ev_ptfree[ps]])
                    if ev_pend is not None:
                        emit_pvsig(*ev_pend)
                    ev_pend = (t, h, hk, kt, c0, W, ps, ev_exp,
                               kt == 0, kt == ktmax - 1)

                    def emit_pvsig(t_, h_, hk_, kt_, c0_, W_, ps_, ev_e, st0, stN):
                        nonlocal ev_pv_last, ev_sg_last
                        w1 = [ev_e]
                        if st0:
                            w1 += [ev_pv_free]
                        ev_pv_last = te(lambda e: e.matmul(
                            PA[:, t_ * 512 + c0_:(t_ + 1) * 512],
                            vt[:, kt_ * 512 + hk_ * 128:kt_ * 512 + (hk_ + 1) * 128],
                            a3[:, t_ * 1024 + ps_ * 512:t_ * 1024 + ps_ * 512 + W_],
                            start=st0, stop=stN, skip_group_check=True), w1)
                        w2 = [ev_sg_free] if st0 else []
                        ev_sg_last = te(lambda e: e.matmul(
                            PC[0:1, t_ * 512 + c0_:(t_ + 1) * 512], onec[:],
                            a3[:, t_ * 1024 + ps_ * 512:t_ * 1024 + ps_ * 512 + W_],
                            start=st0, stop=stN, skip_group_check=True), w2)
                        ev_ptfree[ps_] = ev_sg_last
                    yield
                emit_pvsig(*ev_pend)
                ev_pend = None
                first = False
                n0 = ve(lambda e, tln=tln, sg=sg: e.tensor_scalar(
                    tln[0:1, :], sg, 1.0, 1.0e-36, ALU.mult, ALU.add),
                        [ev_sg_last])
                n1a = se(lambda e, tln=tln, t=t: e.activation(
                    lnrow[t][:], tln[0:1, :], ACT.Ln), [n0])
                sgap()
                n1 = se(lambda e, rsg=rsg, t=t: e.activation(
                    rsg[0:1, :], lnrow[t][:], ACT.Exp, scale=-1.0), [n1a])
                n2 = te(lambda e, stp=stp, rsg=rsg: e.matmul(
                    stp[:], oner32[:], rsg[0:1, :], start=True, stop=True),
                        [n1, ev_exp, ev_oner32])
                n3 = ve(lambda e, rsg=rsg, stp=stp: e.tensor_copy(rsg, stp[:]),
                        [n2])
                vgap()
                n5 = ve(lambda e, h=h, s4=s4, pv=pv, rsg=rsg: e.tensor_tensor(
                    qa[:, h * OWN + s4 * 512:h * OWN + (s4 + 1) * 512],
                    pv, rsg, ALU.mult), [n3, ev_pv_last])
                ev_attn_w[(h, s4)] = n5
                ev_st_free = n3
                ev_pv_free = n5
                ev_sg_free = n0
                yield

    ev_attn_w = {}
    gens = [stream_gen(0), stream_gen(1)]
    done = [False, False]
    for _ in range(9):
        next(gens[0])
    it = 0
    while not all(done):
        for t in range(2):
            if not done[t]:
                try:
                    next(gens[t])
                except StopIteration:
                    done[t] = True
        it += 1
        if it % 37 == 0:
            if n_aliss[0] - n_alred[0] >= 2:
                alpha_reduce()
            if n_aliss[0] < 16:
                alpha_issue()
    while n_alred[0] < 16:
        if n_aliss[0] < 16 and n_aliss[0] - n_alred[0] < 2:
            alpha_issue()
        else:
            alpha_reduce()
    ev_attn_done_ve = [ev_attn_w[(2 * 7 + t, 3)] for t in range(2)]
    ev_attn_all = list(ev_attn_done_ve)
    ev_pa_free = ev_attn_done_ve[1]
    ev_pb_free = ev_attn_done_ve[1]
    ev_pc_free = ev_attn_done_ve[1]
    ev_last_pv = ev_attn_done_ve[1]

    if debug:
        dg("dbg", d_asum, asum[:], [ev_acc])
    ev_am = te(lambda e: e.matmul(PB[0:1, 0:1], onec32[:], asum[:],
                                  start=True, stop=True),
               [ev_acc, ev_onec32] + ev_attn_done_ve)
    ev_at = ve(lambda e: e.tensor_copy(atot[:], PB[0:1, 0:1]), [ev_am])
    ev_ab = te(lambda e: e.matmul(PB[:, 0:1], oner32[:], atot[:],
                                  start=True, stop=True), [ev_at, ev_oner32])
    ev_a1 = ve(lambda e: e.tensor_scalar(al_a[:], PB[:, 0:1],
                                         1.0 / (float(D) * float(D)), 1.0e-8,
                                         ALU.mult, ALU.max), [ev_ab])
    vgap()
    ev_al128 = ve(lambda e: e.reciprocal(al_r[:], al_a[:]), [ev_a1])
    ev_pb_free = ev_a1

    ev_wslab_rd = [None, None]
    ev_x2_rd_prev = None
    ev_evs_rd = None
    ev_pc2_rd = ev_pc_free
    ev_xq_free = None
    ev_wfq_rd = [None, None]
    ev_pab_free = [ev_pa_free, ev_pb_free]

    ev_x2_store = []
    for half in range(2):
        ev_xo = []
        for t8 in range(8):
            tile = half * 8 + t8
            w = list(ev_x2_store) + (ev_attn_all if half == 0 else [])
            src = xo[tile * 128:(tile + 1) * 128, :]
            dstp = x2f[:, t8 * 2048:(t8 + 1) * 2048]
            if t8 < 4:
                ev_xo.append(dg("lp1", dstp, src, w))
            elif t8 < 6:
                ev_xo.append(dsc("lp0", dstp, src, w))
            else:
                ev_xo.append(ds("lxp", dstp, src, w))
        pab = [PA, PB]
        ev_x2add = None
        for o in range(16):
            sb = o % 2
            ev_sl = dg(f"lw{sb}", wqs[sb][:], wp_t[o * 128:(o + 1) * 128, :],
                       [ev_wslab_rd[sb] if not (half == 0 and o < 2) else ev_qslab_rd[sb]])
            last = None
            for d in range(16):
                w = []
                if d == 0:
                    w = [ev_sl, ev_pab_free[sb]]
                    w += ev_attn_all if (half == 0 and o == 0) else []
                for (r0, r1) in ((0, 512), (512, 1024)):
                    last = te(lambda e, d=d, sb=sb, o=o, r0=r0, r1=r1, half=half:
                              e.matmul(
                        pab[sb][:, r0:r1], wqs[sb][:, d * 128:(d + 1) * 128],
                        qa[:, d * OWN + half * 1024 + r0:d * OWN + half * 1024 + r1],
                        start=(d == 0), stop=(d == 15), skip_group_check=True),
                        w if (r0 == 0 and d == 0) else [])
            ev_wslab_rd[sb] = last
            ev_evac = se(lambda e, sb=sb: e.activation(
                evsb[:, 0:1024], pab[sb][:], ACT.Copy), [last, ev_evs_rd])
            ev_pab_free[sb] = ev_evac
            last_t = None
            for i in range(8):
                w = [ev_evac] if i > 0 else [ev_evac, ev_pc2_rd,
                                             ev_idf if half == 0 and o == 0 else None]
                last_t = te(lambda e, i=i: e.transpose(
                    PC[:, i * 128:(i + 1) * 128],
                    evsb[:, i * 128:(i + 1) * 128], idf[:]), w)
            ev_evs_rd = last_t
            x2ap = x2f.rearrange("p (t c) -> p t c", c=2048)[:, :, o * 128:(o + 1) * 128]
            ev_x2add = ve(lambda e, x2ap=x2ap: e.tensor_tensor(
                x2ap, PC[:].rearrange("p (t c) -> p t c", c=128), x2ap, ALU.add),
                [last_t] + (ev_xo if o == 0 else []))
            ev_pc2_rd = ev_x2add
            ev_x2_last_w = ev_x2add

        ev_dbg_x2 = None
        if debug and half == 0:
            e1 = dg("dbg", d_attn, qa[:, :], ev_attn_all)
            e2 = dg("dbg", d_al, al_a[:], [ev_al128, e1])
            e2 = dg("dbg", d_alr, al_r[:], [ev_al128, e2])
            ev_dbg_x2 = dg("dbg", d_x2a, x2f, [ev_x2add, e2])
        ev_xq_tile = []
        ev_ts3_rd = None
        ev_stage_free = [None, None]
        for t8 in range(8):
            buf = t8 % 2
            stg = evsb[:, 0:2048] if buf == 0 else xbuf[:, 0:2048]
            src = x2f[:, t8 * 2048:(t8 + 1) * 2048]
            ev_ln = emit_ln(buf, src, ev_x2add, stg,
                            [ev_lnrd[buf], ev_evs_rd if t8 == 0 else None,
                             ev_stage_free[buf],
                             (ev_albuf_rd[0] if half == 0 and t8 == 1 else None),
                             (ev_albuf_rd[1] if half == 0 and t8 == 1 else None)])
            ev_lnrd[buf] = ev_ln
            vgap()
            ev_bm = ve(lambda e, stg=stg, buf=buf: e.reduce_max(
                bmax2[buf][:], stg, AX, apply_absolute_value=True), [ev_ln])
            vgap()
            ev_be = ve(lambda e, t8=t8, buf=buf: e.tensor_scalar(
                beta8[:, t8:t8 + 1], bmax2[buf][:], 1.0 / 127.0, 1.0e-8,
                ALU.mult, ALU.max), [ev_bm])
            vgap()
            ev_br = ve(lambda e, t8=t8: e.reciprocal(betr8[:, t8:t8 + 1],
                                                     beta8[:, t8:t8 + 1]), [ev_be])
            vgap()
            q1 = ve(lambda e, t8=t8, stg=stg: e.tensor_scalar(
                stg, stg, betr8[:, t8:t8 + 1], MAGIC,
                ALU.mult, ALU.add), [ev_br])
            vgap()
            q2 = ve(lambda e, stg=stg: e.tensor_scalar(
                stg, stg, MAGIC, 127.0,
                ALU.subtract, ALU.min), [q1])
            vgap()
            q3 = ve(lambda e, stg=stg, buf=buf: e.tensor_scalar(
                wqs[buf][:, 0:2048], stg, -127.0, None,
                ALU.max), [q2, ev_ts3_rd,
                           ev_wslab_rd[buf] if t8 < 2 else None])
            for rnd in range(2):
                pb = ptb_next()
                last_t = None
                for i in range(8):
                    w = [q3, ev_ptbf[pb]] if i == 0 else []
                    if t8 == 0 and rnd == 0 and i == 0:
                        w.append(ev_last_pv if half == 0 else ev_xq_free)
                    last_t = te(lambda e, i=i, rnd=rnd, pb=pb, buf=buf: e.transpose(
                        ptbs[pb][:, i * 128:(i + 1) * 128],
                        wqs[buf][:, (rnd * 8 + i) * 128:(rnd * 8 + i + 1) * 128],
                        idb[:]), w)
                dst_ap = a3[:, rnd * 8192:(rnd + 1) * 8192] \
                    .rearrange("p (i c) -> p i c", c=1024)[:, :, t8 * 128:(t8 + 1) * 128]
                ev_ptbf[pb] = se(lambda e, dst=dst_ap, pb=pb: e.activation(
                    dst, ptbs[pb][:].rearrange("p (i c) -> p i c", c=128),
                    ACT.Copy), [last_t])
                ev_ts3_rd = ev_ptbf[pb]
            ev_stage_free[buf] = ev_ts3_rd
            ev_xq_tile.append(ev_ts3_rd)
        ev_ffn_done_tile = [list() for _ in range(8)]
        ev_q3 = {}

        def emit_wquant(o):
            buf = o % 2
            sbq = o % 2
            ev_wl = dg(f"lg{buf}", xbuf[:, buf * D:(buf + 1) * D],
                       wf_t[o * 128:(o + 1) * 128, :],
                       [ev_albuf_rd[buf] if half == 0 and o < 2 else None,
                        ev_wfq_rd[buf],
                        ev_stage_free[1] if o < 2 else None])
            wlast = None
            for hc in range(2):
                w1 = ve(lambda e, buf=buf, hc=hc: e.tensor_scalar(
                    evsb[:, 1024:2048],
                    xbuf[:, buf * D + hc * 1024:buf * D + (hc + 1) * 1024],
                    al_r[:], 1.25, ALU.mult, ALU.min),
                    [ev_wl, ev_al128,
                     ev_ts3_rd if (o == 0 and hc == 0) else None, wlast])
                vgap()
                w2 = ve(lambda e: e.tensor_scalar(
                    evsb[:, 1024:2048], evsb[:, 1024:2048], -1.25, MAGIC,
                    ALU.max, ALU.add), [w1])
                vgap()
                wlast = ve(lambda e, sbq=sbq, hc=hc: e.tensor_scalar(
                    wqs[sbq][:, hc * 1024:(hc + 1) * 1024], evsb[:, 1024:2048],
                    MAGIC, None, ALU.subtract),
                    [w2, (ev_wslab_rd[sbq] if o >= 2 else ev_stage_free[sbq])
                     if hc == 0 else None])
            ev_wfq_rd[buf] = wlast
            ev_q3[o] = wlast

        emit_wquant(0)
        for o in range(16):
            if o + 1 < 16:
                emit_wquant(o + 1)
            sbq = o % 2
            w3 = ev_q3[o]
            if debug and half == 0 and o == 0:
                w3 = dg("dbg", d_wq, wqs[sbq][:, 0:2048], [w3])
            last = None
            for d in range(16):
                w = [w3, ev_pab_free[o % 2]] + (ev_xq_tile if o == 0 and d == 0 else [])
                for (r0, r1) in ((0, 512), (512, 1024)):
                    last = te(lambda e, d=d, o=o, r0=r0, r1=r1, sbq=sbq: e.matmul(
                        pab[o % 2][:, r0:r1], wqs[sbq][:, d * 128:(d + 1) * 128],
                        a3[:, d * 1024 + r0:d * 1024 + r1],
                        start=(d == 0), stop=(d == 15), skip_group_check=True),
                        w if (d == 0 and r0 == 0) else [])
            ev_wslab_rd[sbq] = last
            ev_evac = se(lambda e, o=o: e.activation(
                evsb[:, 0:1024], pab[o % 2][:], ACT.Copy, scale=al_a[:]),
                [last, ev_evs_rd, ev_al128])
            ev_pab_free[o % 2] = ev_evac
            if debug and half == 0 and o == 0:
                ev_evac = dg("dbg", d_ev, evsb[:, 0:1024], [ev_evac])
            last_t = None
            for i in range(8):
                w = [ev_evac] if i > 0 else [ev_evac, ev_pc2_rd]
                last_t = te(lambda e, i=i: e.transpose(
                    PC[:, i * 128:(i + 1) * 128], evsb[:, i * 128:(i + 1) * 128],
                    idf[:]), w)
            ev_evs_rd = last_t
            for t8 in range(8):
                evb = ve(lambda e, t8=t8, o=o: e.scalar_tensor_tensor(
                    x2f[:, t8 * 2048 + o * 128:t8 * 2048 + (o + 1) * 128],
                    PC[:, t8 * 128:(t8 + 1) * 128], beta8[:, t8:t8 + 1],
                    x2f[:, t8 * 2048 + o * 128:t8 * 2048 + (o + 1) * 128],
                    ALU.mult, ALU.add),
                    [last_t] + ([ev_dbg_x2] if debug and half == 0 and o == 0 and t8 == 0 else []))
                ev_ffn_done_tile[t8].append(evb)
            ev_pc2_rd = ev_ffn_done_tile[7][-1]
        evA = evB = None
        for t8 in range(8):
            tile = half * 8 + t8
            w = [ev_ffn_done_tile[t8][-1]]
            if t8 % 2 == 0:
                evA = dst(y[tile * 128:(tile + 1) * 128, :],
                          x2f[:, t8 * 2048:(t8 + 1) * 2048], w)
            else:
                evB = dsc("sts2", y[tile * 128:(tile + 1) * 128, :],
                          x2f[:, t8 * 2048:(t8 + 1) * 2048], w)
        ev_x2_store = [evA, evB]
        ev_xq_free = last

    total_sts = sch.cnt["sts"]
    total_sts2 = sch.cnt["sts2"]

    semnums = sorted(sems[k].num for k in sems)
    assert semnums[-1] - semnums[0] == len(semnums) - 1, semnums
    nc.gpsimd.sem_clear(range(semnums[0], semnums[-1] + 1))
    nc.all_engine_barrier()
    with nc.allow_low_precision(reason="audited bf16/fp32 mixed pipeline"):
        with nc.Block() as block:
            @block.sync
            def _(sy):
                sch.emit("sync", sy, sems, set())
                sy.wait_ge(sems["sts"], total_sts)
                sy.wait_ge(sems["sts2"], total_sts2)

            @block.gpsimd
            def _(g):
                sch.emit("gpsimd", g, sems, set())

            @block.tensor
            def _(t):
                sch.emit("tensor", t, sems, {"te"})

            @block.vector
            def _(v):
                sch.emit("vector", v, sems, {"ve"})

            @block.scalar
            def _(s):
                sch.emit("scalar", s, sems, {"se"})

    ctx.pop_all()
    return nc


_CACHE = {}


def _tables():
    if "t" in _CACHE:
        return _CACHE["t"]
    half = 64
    inv = 1.0 / (10000.0 ** (np.arange(half, dtype=np.float64) / half))
    pos = np.arange(T, dtype=np.float64)
    ang = pos[:, None] * inv[None, :]
    cos = np.cos(ang)
    sin = np.sin(ang)
    ropek_flat = np.concatenate([cos, -sin, sin], axis=1).astype(np.float32)
    ropek = np.ascontiguousarray(
        ropek_flat.reshape(NKT, 128, 192).transpose(1, 0, 2).reshape(128, NKT * 192))
    rq_cc = {}
    rq_ss = {}
    rows = {}
    for p in range(2):
        rp = np.concatenate([np.arange(g * 128, (g + 1) * 128)
                             for g in range(p, NKT, 2)])
        rows[p] = rp
        c = cos[rp].T
        s = sin[rp].T
        rq_cc[p] = (np.concatenate([c, c], axis=0) * SCALE).astype(np.float32)
        rq_ss[p] = (np.concatenate([-s, s], axis=0) * SCALE).astype(np.float32)
    tri = np.where(np.arange(128)[:, None] > np.arange(128)[None, :],
                   np.float32(NEG), np.float32(0.0))
    full = np.full((128, 128), np.float32(NEG))
    zero = np.zeros((128, 128), np.float32)
    dmask = {}
    for p in range(2):
        slabs = []
        for j in range(NOT_):
            if p == 0:
                slabs.append(np.concatenate([tri, full], axis=1))
            else:
                slabs.append(np.concatenate([zero, tri], axis=1))
        dmask[p] = np.ascontiguousarray(
            np.concatenate(slabs, axis=0).reshape(NOT_, 128, 256)
            .transpose(1, 0, 2).reshape(128, NOT_ * 256))
    _CACHE["t"] = (ropek, rq_cc, rq_ss, dmask, rows)
    return _CACHE["t"]


def blockT(w):
    wt = np.ascontiguousarray(w.T)
    return np.ascontiguousarray(
        wt.reshape(16, 128, 16, 128).transpose(2, 1, 0, 3).reshape(2048, 2048))


def kernel(L0_attn_x, L0_attn_Wq, L0_attn_Wk, L0_attn_Wv, L0_attn_Wp,
           L0_ffn_weight):
    x = np.ascontiguousarray(np.asarray(L0_attn_x, dtype=np.float32))
    wq = np.asarray(L0_attn_Wq, dtype=np.float32)
    wk = np.asarray(L0_attn_Wk, dtype=np.float32)
    wv = np.asarray(L0_attn_Wv, dtype=np.float32)
    wp = np.asarray(L0_attn_Wp, dtype=np.float32)
    wf = np.asarray(L0_ffn_weight, dtype=np.float32)

    wq_t = blockT(wq)
    wp_t = blockT(wp)
    wf_t = blockT(wf)
    wkvT = np.concatenate([wk.T, wv.T], axis=1)
    wkv_t = np.ascontiguousarray(
        wkvT.reshape(16, 128, 1024).transpose(1, 0, 2).reshape(128, 16384))
    ropek, rq_cc, rq_ss, dmask, rows = _tables()
    onescol = np.ones((128, 1), np.float32)
    onesrow = np.ones((1, 128), np.float32)
    identin = np.eye(128, dtype=np.float32)

    if "nc" not in _CACHE:
        _CACHE["nc"] = build_program()
    nc = _CACHE["nc"]

    in_maps = []
    for c in range(8):
        b, p = c // 2, c % 2
        in_maps.append(dict(
            xk=np.ascontiguousarray(x[b]),
            xo=np.ascontiguousarray(x[b][rows[p]]),
            wq_t=wq_t, wkv_t=wkv_t, wp_t=wp_t, wf_t=wf_t,
            ropeq_cc=rq_cc[p], ropeq_ss=rq_ss[p], ropek=ropek,
            dmaskd=dmask[p], onescol=onescol, onesrow=onesrow,
            identin=identin,
        ))
    res = run_bass_kernel_spmd(nc, in_maps, core_ids=list(range(8)))
    out = np.empty((NB, T, D), np.float32)
    for c in range(8):
        b, p = c // 2, c % 2
        out[b][rows[p]] = res.results[c]["y"]
    return out
